# version 03
# baseline (speedup 1.0000x reference)
import numpy as np
import ml_dtypes
from contextlib import ExitStack

import concourse.bass as bass
import concourse.mybir as mybir
from concourse.bass_utils import run_bass_kernel_spmd

F32 = mybir.dt.float32
BF16 = mybir.dt.bfloat16
I8 = mybir.dt.int8
AF = mybir.ActivationFunctionType
ALU = mybir.AluOpType
AX = mybir.AxisListType
ts = bass.ts
bfdt = ml_dtypes.bfloat16

P = 128
T = 2048
D = 2048
HID = 8192
NT = T // P
ND = D // P
EPS = 1e-5
SC = 1.0 / np.sqrt(128.0)
NCORES = 8
ALLE = mybir.ALL_ENGINES


def _had(n):
    h = np.array([[1.0]])
    while h.shape[0] < n:
        h = np.block([[h, h], [h, -h]])
    return h


def _consts():
    c = {}
    c["ident"] = np.eye(P, dtype=bfdt)
    c["m0"] = np.where(np.arange(P)[None, :] <= np.arange(P)[:, None], 0.0,
                       -1.0e9).astype(np.float32)
    c["b0"] = np.broadcast_to(np.arange(T, dtype=np.float32), (P, T)).copy()
    c["h128"] = _had(128).astype(bfdt)
    half = 64
    inv = 10000.0 ** (-np.arange(half, dtype=np.float64) / half)
    ang = inv[:, None] * np.arange(T, dtype=np.float64)[None, :]
    cc = np.concatenate([np.cos(ang), np.cos(ang)], 0)
    ss = np.concatenate([-np.sin(ang), np.sin(ang)], 0)
    c["cc"] = cc.astype(np.float32)
    c["ss"] = ss.astype(np.float32)
    c["ccq"] = (cc * SC).astype(np.float32)
    c["ssq"] = (ss * SC).astype(np.float32)
    return c


class Prog:
    def __init__(self):
        self.nc = bass.Bass("TRN2", target_bir_lowering=False,
                            detect_race_conditions=False)
        self.es = ExitStack()

    def inp(self, name, shape, dt=F32):
        return self.nc.dram_tensor(name, list(shape), dt, kind="ExternalInput")

    def out(self, name, shape, dt=F32):
        return self.nc.dram_tensor(name, list(shape), dt, kind="ExternalOutput")

    def sb(self, name, shape, dt):
        return self.es.enter_context(self.nc.sbuf_tensor(name, list(shape), dt))

    def ps(self, name, shape, dt=F32):
        return self.es.enter_context(self.nc.psum_tensor(name, list(shape), dt))

    def sem(self, name):
        return self.es.enter_context(self.nc.semaphore(name))


def build_mm_phase(din, dout_chunks, wdt, epi, tiles=1, has_bias=False,
                   scale_imm=None, act_scale=False, act_fn="copy"):
    rope = (epi == "rope")
    p = Prog()
    nc = p.nc
    nki = din // P
    TT_ = T // tiles
    xn = p.inp("xn", [din, T], BF16)
    w = p.inp("w", [din, dout_chunks * P], wdt)
    w3 = w.rearrange("(i p) o -> p i o", p=P)
    dst = p.out("o", [dout_chunks * P, T], BF16)
    bias_h = p.inp("bias", [P, dout_chunks], F32) if has_bias else None
    scol_h = p.inp("scol", [P, dout_chunks], F32) if act_scale else None
    cc_h = p.inp("cc", [P, T], F32) if rope else None
    ss_h = p.inp("ss", [P, T], F32) if rope else None

    xnb = p.sb("xnb", [P, nki * TT_], BF16)
    wb = [p.sb(f"wb{i}", [P, nki * P], BF16) for i in range(2)]
    tmpf = [p.sb(f"tmpf{i}", [P, TT_], F32) for i in range(2)]
    ob = [p.sb(f"ob{i}", [P, TT_], BF16) for i in range(2)]
    psb = [p.ps(f"psb{i}", [P, min(TT_, 2048)], F32) for i in range(2)]
    bias_sb = p.sb("bias_sb", [P, dout_chunks], F32)
    scol_sb = p.sb("scol_sb", [P, dout_chunks], F32)
    if rope:
        cc_sb = p.sb("cc_sb", [P, T], F32)
        ss_sb = p.sb("ss_sb", [P, T], F32)
        rtmp = p.sb("rtmp", [P, T], F32)
        rtm2 = p.sb("rtm2", [P, T], F32)

    s_x = p.sem("s_x")
    s_w = [p.sem(f"s_w{i}") for i in range(2)]
    s_mm = p.sem("s_mm")
    s_se = p.sem("s_se")
    s_g = p.sem("s_g")
    s_st = p.sem("s_st")

    jobs = [(tt, o) for tt in range(tiles) for o in range(dout_chunks)]
    njob = len(jobs)
    nconst = (1 if has_bias else 0) + (1 if act_scale else 0) + (2 if rope else 0)
    g_is_epi = epi in ("bias_scale", "rope")

    with p.es, nc.Block() as block:
        @block.gpsimd
        def _(g_):
            if has_bias:
                g_.dma_start(bias_sb[:, :], bias_h[:, :]).then_inc(s_x, 16)
            if act_scale:
                g_.dma_start(scol_sb[:, :], scol_h[:, :]).then_inc(s_x, 16)
            if rope:
                g_.dma_start(cc_sb[:, :], cc_h[:, :]).then_inc(s_x, 16)
                g_.dma_start(ss_sb[:, :], ss_h[:, :]).then_inc(s_x, 16)
            nx = 0
            for k, (tt, o) in enumerate(jobs):
                par = k % 2
                if o == 0:
                    if k > 0:
                        g_.wait_ge(s_se, k)
                    g_.dma_start(
                        xnb[:, :].rearrange("p (i t) -> p i t", i=nki),
                        xn.rearrange("(i p) t -> p i t", p=P)[:, :, ts(tt, TT_)]
                    ).then_inc(s_x, 16)
                    nx += 1
                if k >= 2:
                    g_.wait_ge(s_mm, k - 1)
                g_.dma_start(wb[par][:, :].rearrange("p (i c) -> p i c", i=nki),
                             w3[:, :, ts(o, P)]).then_inc(s_w[par], 16)
                if g_is_epi:
                    g_.wait_ge(s_se, k + 1)
                    if k >= 2:
                        g_.wait_ge(s_st, 16 * (k - 1))
                    src = tmpf[par]
                    if epi == "bias_scale":
                        if has_bias:
                            ge = g_.tensor_scalar(ob[par][:, :], src[:, :],
                                                  bias_sb[:, o:o + 1],
                                                  scale_imm or 1.0, ALU.add, ALU.mult)
                        else:
                            ge = g_.tensor_scalar(ob[par][:, :], src[:, :],
                                                  scale_imm or 1.0, 1.0,
                                                  ALU.mult, ALU.mult)
                    else:
                        g_.tensor_scalar(rtmp[:, :], src[:, :],
                                         bias_sb[:, o:o + 1], 1.0, ALU.add, ALU.mult)
                        g_.tensor_copy(rtm2[0:64, :], rtmp[64:128, :])
                        g_.tensor_copy(rtm2[64:128, :], rtmp[0:64, :])
                        g_.tensor_mul(rtmp[:, :], rtmp[:, :], cc_sb[:, :])
                        g_.tensor_mul(rtm2[:, :], rtm2[:, :], ss_sb[:, :])
                        ge = g_.tensor_add(ob[par][:, :], rtmp[:, :], rtm2[:, :])
                    ge.then_inc(s_g, 1)

        @block.tensor
        def _(t_):
            nx = 0
            for k, (tt, o) in enumerate(jobs):
                par = k % 2
                if o == 0:
                    nx += 1
                t_.wait_ge(s_x, 16 * (nconst + nx))
                t_.wait_ge(s_w[par], 16 * (k // 2 + 1))
                if k >= 2:
                    t_.wait_ge(s_se, k - 1)
                nn = (TT_ + 511) // 512
                for n in range(nn):
                    w_ = min(512, TT_ - n * 512)
                    for i in range(nki):
                        mm = t_.matmul(psb[par][:, n * 512:n * 512 + w_],
                                       wb[par][:, i * P:(i + 1) * P],
                                       xnb[:, i * TT_ + n * 512: i * TT_ + n * 512 + w_],
                                       start=(i == 0), stop=(i == nki - 1))
                mm.then_inc(s_mm, 1)

        @block.scalar
        def _(s_):
            for k, (tt, o) in enumerate(jobs):
                par = k % 2
                s_.wait_ge(s_mm, k + 1)
                if g_is_epi:
                    if k >= 2:
                        s_.wait_ge(s_g, k - 1)
                    s_.copy(tmpf[par][:, :], psb[par][:, :]).then_inc(s_se, 1)
                else:
                    if k >= 2:
                        s_.wait_ge(s_st, 16 * (k - 1))
                    if act_scale:
                        fn = AF.Silu if act_fn == "silu" else AF.Copy
                        s_.activation(ob[par][:, :], psb[par][:, :], fn,
                                      scale=scol_sb[:, o:o + 1]).then_inc(s_se, 1)
                    else:
                        s_.copy(ob[par][:, :], psb[par][:, :]).then_inc(s_se, 1)

        @block.sync
        def _(sy):
            for k, (tt, o) in enumerate(jobs):
                par = k % 2
                sy.wait_ge(s_g if g_is_epi else s_se, k + 1)
                if tiles == 1:
                    sy.dma_start(dst[ts(o, P), :], ob[par][:, :]).then_inc(s_st, 16)
                else:
                    sy.dma_start(dst[ts(o, P), ts(tt, TT_)],
                                 ob[par][:, :]).then_inc(s_st, 16)
            sy.wait_ge(s_st, 16 * njob)
    return p


def build_vtm(dout, nkvh):
    p = Prog()
    nc = p.nc
    nki = ND
    xn = p.inp("xn", [D, T], BF16)
    w = p.inp("w", [D, dout], F32)
    vb = p.inp("vbias", [P, dout], F32)
    dst = p.out("vtm", [nkvh, T, P], BF16)
    w3 = w.rearrange("(i p) o -> p i o", p=P)

    xnb = p.sb("xnb", [P, nki * T], BF16)
    wball = p.sb("wball", [P, nki * dout], BF16)
    vb_sb = p.sb("vb_sb", [P, dout], F32)
    tmpf = [p.sb(f"tmpf{i}", [P, dout], F32) for i in range(2)]
    ob = [p.sb(f"ob{i}", [P, dout], BF16) for i in range(2)]
    nn = (dout + 511) // 512
    psb = [p.ps(f"psb{i}", [P, min(dout, 2048)], F32) for i in range(2)]

    s_x = p.sem("s_x"); s_mm = p.sem("s_mm"); s_se = p.sem("s_se")
    s_g = p.sem("s_g"); s_st = p.sem("s_st")

    with p.es, nc.Block() as block:
        @block.gpsimd
        def _(g_):
            g_.dma_start(vb_sb[:, :], vb[:, :]).then_inc(s_x, 16)
            g_.dma_start(xnb[:, :].rearrange("p (i t) -> p i t", i=nki),
                         xn.rearrange("(i p) t -> p i t", p=P)).then_inc(s_x, 16)
            g_.dma_start(wball[:, :].rearrange("p (i c) -> p i c", i=nki),
                         w3[:, :, :]).then_inc(s_x, 16)
            for k in range(NT):
                par = k % 2
                g_.wait_ge(s_se, k + 1)
                if k >= 2:
                    g_.wait_ge(s_st, 16 * nkvh * (k - 1))
                g_.tensor_add(ob[par][:, :], tmpf[par][:, :],
                              vb_sb[:, :]).then_inc(s_g, 1)

        @block.tensor
        def _(t_):
            t_.wait_ge(s_x, 48)
            for k in range(NT):
                par = k % 2
                if k >= 2:
                    t_.wait_ge(s_se, k - 1)
                for n in range(nn):
                    w_ = min(512, dout - n * 512)
                    for i in range(nki):
                        mm = t_.matmul(psb[par][:, n * 512:n * 512 + w_],
                                       xnb[:, i * T + k * P: i * T + (k + 1) * P],
                                       wball[:, i * dout + n * 512: i * dout + n * 512 + w_],
                                       start=(i == 0), stop=(i == nki - 1))
                mm.then_inc(s_mm, 1)

        @block.scalar
        def _(s_):
            for k in range(NT):
                par = k % 2
                s_.wait_ge(s_mm, k + 1)
                if k >= 2:
                    s_.wait_ge(s_g, k - 1)
                s_.copy(tmpf[par][:, :], psb[par][:, :]).then_inc(s_se, 1)

        @block.sync
        def _(sy):
            for k in range(NT):
                par = k % 2
                sy.wait_ge(s_g, k + 1)
                for h in range(nkvh):
                    sy.dma_start(dst[h, ts(k, P), :],
                                 ob[par][:, h * P:(h + 1) * P]).then_inc(s_st, 16)
            sy.wait_ge(s_st, 16 * nkvh * NT)
    return p


def build_gateup(wdt, with_scales):
    p = Prog()
    nc = p.nc
    nki = ND
    NH = HID // P
    xn = p.inp("xn", [D, T], BF16)
    wg = p.inp("wg", [D, HID], wdt)
    wu = p.inp("wu", [D, HID], wdt)
    dst = p.out("fuse", [HID, T], BF16)
    wg3 = wg.rearrange("(i p) o -> p i o", p=P)
    wu3 = wu.rearrange("(i p) o -> p i o", p=P)
    if with_scales:
        sg_h = p.inp("sg", [P, NH], F32)
        su_h = p.inp("su", [P, NH], F32)

    xnb = p.sb("xnb", [P, nki * T], BF16)
    wgb = [p.sb(f"wgb{i}", [P, nki * P], BF16) for i in range(2)]
    wub = [p.sb(f"wub{i}", [P, nki * P], BF16) for i in range(2)]
    gsb = [p.sb(f"gsb{i}", [P, 1024], F32) for i in range(2)]
    usb = [p.sb(f"usb{i}", [P, 1024], F32) for i in range(2)]
    fsb = [p.sb(f"fsb{i}", [P, T], BF16) for i in range(2)]
    sg_sb = p.sb("sg_sb", [P, NH], F32)
    su_sb = p.sb("su_sb", [P, NH], F32)
    psg = [p.ps(f"psg{i}", [P, 1024], F32) for i in range(2)]
    psu = [p.ps(f"psu{i}", [P, 1024], F32) for i in range(2)]

    s_x = p.sem("s_x"); s_w = [p.sem(f"s_w{i}") for i in range(2)]
    s_mm = p.sem("s_mm"); s_se = p.sem("s_se"); s_ve = p.sem("s_ve")
    s_st = p.sem("s_st")

    nconst = 2 if with_scales else 0

    with p.es, nc.Block() as block:
        @block.gpsimd
        def _(g_):
            if with_scales:
                g_.dma_start(sg_sb[:, :], sg_h[:, :]).then_inc(s_x, 16)
                g_.dma_start(su_sb[:, :], su_h[:, :]).then_inc(s_x, 16)
            g_.dma_start(xnb[:, :].rearrange("p (i t) -> p i t", i=nki),
                         xn.rearrange("(i p) t -> p i t", p=P)).then_inc(s_x, 16)

            def body(ch, first):
                for cpar in range(2):
                    c = ch * 2 + cpar
                    if not first:
                        g_.wait_ge(s_mm, 4 * ch - 2 + 2 * cpar)
                    g_.dma_start(wgb[cpar][:, :].rearrange("p (i c) -> p i c", i=nki),
                                 wg3[:, :, ts(c, P)]).then_inc(s_w[cpar], 16)
                    g_.dma_start(wub[cpar][:, :].rearrange("p (i c) -> p i c", i=nki),
                                 wu3[:, :, ts(c, P)]).then_inc(s_w[cpar], 16)
                for cpar in range(2):
                    for hf in range(2):
                        k = 4 * ch + 2 * cpar + hf
                        c = 2 * ch + cpar
                        g_.wait_ge(s_se, 2 * (k + 1))
                        if not first and hf == 0:
                            g_.wait_ge(s_st, 16 * (c - 1))
                        g_.tensor_mul(fsb[cpar][:, hf * 1024:(hf + 1) * 1024],
                                      gsb[hf][:, :], usb[hf][:, :]).then_inc(s_ve, 1)
            body(0, True)
            with g_.Fori(1, NH // 2) as ch:
                body(ch, False)

        @block.tensor
        def _(t_):
            t_.wait_ge(s_x, 16 * (nconst + 1))

            def job(ch, cpar, hf, first):
                k = 4 * ch + 2 * cpar + hf
                pp = hf
                t_.wait_ge(s_w[cpar], 32 * (ch + 1))
                if not first:
                    t_.wait_ge(s_se, 2 * (k - 1))
                for n in range(2):
                    off = hf * 1024 + n * 512
                    for i in range(nki):
                        t_.matmul(psg[pp][:, n * 512:(n + 1) * 512],
                                  wgb[cpar][:, i * P:(i + 1) * P],
                                  xnb[:, i * T + off: i * T + off + 512],
                                  start=(i == 0), stop=(i == nki - 1))
                for n in range(2):
                    off = hf * 1024 + n * 512
                    for i in range(nki):
                        mm = t_.matmul(psu[pp][:, n * 512:(n + 1) * 512],
                                       wub[cpar][:, i * P:(i + 1) * P],
                                       xnb[:, i * T + off: i * T + off + 512],
                                       start=(i == 0), stop=(i == nki - 1))
                mm.then_inc(s_mm, 1)
            for cpar in range(2):
                for hf in range(2):
                    job(0, cpar, hf, 2 * cpar + hf < 2)
            with t_.Fori(1, NH // 2) as ch:
                for cpar in range(2):
                    for hf in range(2):
                        job(ch, cpar, hf, False)

        @block.scalar
        def _(s_):
            for c in range(NH):
                for hf in range(2):
                    k = 2 * c + hf
                    pp = hf
                    s_.wait_ge(s_mm, k + 1)
                    if k >= 2:
                        s_.wait_ge(s_ve, k - 1)
                    if with_scales:
                        s_.activation(gsb[pp][:, :], psg[pp][:, :], AF.Silu,
                                      scale=sg_sb[:, c:c + 1])
                        s_.activation(usb[pp][:, :], psu[pp][:, :], AF.Copy,
                                      scale=su_sb[:, c:c + 1]).then_inc(s_se, 2)
                    else:
                        s_.activation(gsb[pp][:, :], psg[pp][:, :], AF.Silu)
                        s_.copy(usb[pp][:, :], psu[pp][:, :]).then_inc(s_se, 2)

        @block.sync
        def _(sy):
            def job(ch, cpar):
                c = 2 * ch + cpar
                sy.wait_ge(s_ve, 2 * c + 2)
                sy.dma_start(dst[ts(c, P), :], fsb[cpar][:, :]).then_inc(s_st, 16)
            for cpar in range(2):
                job(0, cpar)
            with sy.Fori(1, NH // 2) as ch:
                for cpar in range(2):
                    job(ch, cpar)
            sy.wait_ge(s_st, 16 * NH)
    return p


def build_down(wdt, with_scale):
    p = Prog()
    nc = p.nc
    NH = HID // P
    TT_ = 512
    tiles = T // TT_
    fuse = p.inp("fuse", [HID, T], BF16)
    wd = p.inp("wd", [HID, D], wdt)
    dst = p.out("y", [D, T], BF16)
    wd3 = wd.rearrange("(i p) o -> p i o", p=P)
    sd_h = p.inp("sd", [P, ND], F32) if with_scale else None

    fub = p.sb("fub", [P, NH * TT_], BF16)
    wdb = [p.sb(f"wdb{i}", [P, NH * P], BF16) for i in range(2)]
    ob = [p.sb(f"ob{i}", [P, TT_], BF16) for i in range(2)]
    sd_sb = p.sb("sd_sb", [P, ND], F32)
    psb = [p.ps(f"psb{i}", [P, TT_], F32) for i in range(2)]

    s_x = p.sem("s_x"); s_w = [p.sem(f"s_w{i}") for i in range(2)]
    s_mm = p.sem("s_mm"); s_se = p.sem("s_se"); s_st = p.sem("s_st")

    nconst = 1 if with_scale else 0
    NOP = ND // 2

    with p.es, nc.Block() as block:
        @block.gpsimd
        def _(g_):
            if with_scale:
                g_.dma_start(sd_sb[:, :], sd_h[:, :]).then_inc(s_x, 16)

            def fload(tt):
                g_.dma_start(fub[:, :].rearrange("p (i t) -> p i t", i=NH),
                             fuse.rearrange("(i p) t -> p i t", p=P)[:, :, ts(tt, TT_)]
                             ).then_inc(s_x, 16)

            def body(tt, op, first):
                for par in range(2):
                    k = tt * ND + op * 2 + par
                    if not first or par == 1:
                        g_.wait_ge(s_mm, k - 1)
                    g_.dma_start(wdb[par][:, :].rearrange("p (i c) -> p i c", i=NH),
                                 wd3[:, :, ts(op * 2 + par, P)]).then_inc(s_w[par], 16)
            fload(0)
            body(0, 0, True)
            with g_.Fori(1, NOP) as op:
                body(0, op, False)
            with g_.Fori(1, tiles) as tt:
                g_.wait_ge(s_se, ND * tt)
                fload(tt)
                with g_.Fori(0, NOP) as op:
                    body(tt, op, False)

        @block.tensor
        def _(t_):
            def body(tt, op, first):
                for par in range(2):
                    k = tt * ND + op * 2 + par
                    t_.wait_ge(s_w[par], 16 * (tt * NOP + op + 1))
                    if not first:
                        t_.wait_ge(s_se, k - 1)
                    for i in range(NH):
                        mm = t_.matmul(psb[par][:, :],
                                       wdb[par][:, i * P:(i + 1) * P],
                                       fub[:, i * TT_:(i + 1) * TT_],
                                       start=(i == 0), stop=(i == NH - 1))
                    mm.then_inc(s_mm, 1)
            t_.wait_ge(s_x, 16 * (nconst + 1))
            body(0, 0, True)
            with t_.Fori(1, NOP) as op:
                body(0, op, False)
            with t_.Fori(1, tiles) as tt:
                t_.wait_ge(s_x, 16 * (nconst + tt + 1))
                with t_.Fori(0, NOP) as op:
                    body(tt, op, False)

        @block.scalar
        def _(s_):
            for tt in range(tiles):
                for o in range(ND):
                    par = o % 2
                    k = tt * ND + o
                    s_.wait_ge(s_mm, k + 1)
                    if k >= 2:
                        s_.wait_ge(s_st, 16 * (k - 1))
                    if with_scale:
                        s_.activation(ob[par][:, :], psb[par][:, :], AF.Copy,
                                      scale=sd_sb[:, o:o + 1]).then_inc(s_se, 1)
                    else:
                        s_.copy(ob[par][:, :], psb[par][:, :]).then_inc(s_se, 1)

        @block.sync
        def _(sy):
            def body(tt, op):
                for par in range(2):
                    k = tt * ND + op * 2 + par
                    sy.wait_ge(s_se, k + 1)
                    sy.dma_start(dst[ts(op * 2 + par, P), ts(tt, TT_)],
                                 ob[par][:, :]).then_inc(s_st, 16)
            with sy.Fori(0, NOP) as op:
                body(0, op)
            with sy.Fori(1, tiles) as tt:
                with sy.Fori(0, NOP) as op:
                    body(tt, op)
            sy.wait_ge(s_st, 16 * tiles * ND)
    return p


def build_w2prep():
    p = Prog()
    nc = p.nc
    CT = 256
    nct = D // CT
    NH = HID // P
    w2t = p.inp("w2t", [HID, D], F32)
    h128 = p.inp("h128", [P, P], BF16)
    dst = p.out("w2pt", [HID, D], BF16)
    w3 = w2t.rearrange("(c p) o -> p c o", p=P)
    d3 = dst.rearrange("(c p) o -> p c o", p=P)

    win = p.sb("win", [P, NH * CT], BF16)
    ta = p.sb("ta", [P, NH * CT], F32)
    tb = p.sb("tb", [P, NH * CT], F32)
    tob = p.sb("tob", [P, NH * CT], BF16)
    h_sb = p.sb("h_sb", [P, P], BF16)
    psq = p.ps("psq", [P, 2048], F32)

    s_d = p.sem("s_d")
    s_o = p.sem("s_o")

    with p.es:
        g = nc.gpsimd; te = nc.tensor; s = nc.scalar; sy = nc.sync
        v = nc.vector
        g.dma_start(h_sb[:, :], h128[:, :]).then_inc(s_d, 16)
        g.wait_ge(s_d, 16)
        nc.all_engine_barrier()
        with nc.Fori(0, nct, engines=ALLE) as ct:
            g.dma_start(win[:, :].rearrange("p (c k) -> p c k", c=NH),
                        w3[:, :, ts(ct, CT)]).then_inc(s_d, 16)
            g.wait_ge(s_d, (ct + 2) * 16)
            nc.all_engine_barrier()
            for rnd in range(NH // 8):
                for j in range(8):
                    c = rnd * 8 + j
                    te.matmul(psq[:, j * CT:(j + 1) * CT], h_sb[:, :],
                              win[:, c * CT:(c + 1) * CT], start=True, stop=True)
                nc.all_engine_barrier()
                for j in range(8):
                    c = rnd * 8 + j
                    s.copy(ta[:, c * CT:(c + 1) * CT], psq[:, j * CT:(j + 1) * CT])
                nc.all_engine_barrier()
            src, dsts = ta, tb
            for st in range(6):
                hc = 1 << st
                sv_ = src[:, :].rearrange("p (g two j k) -> p g two j k",
                                          two=2, j=hc, k=CT)
                dv_ = dsts[:, :].rearrange("p (g two j k) -> p g two j k",
                                           two=2, j=hc, k=CT)
                v.tensor_add(dv_[:, :, 0, :, :], sv_[:, :, 0, :, :], sv_[:, :, 1, :, :])
                v.tensor_sub(dv_[:, :, 1, :, :], sv_[:, :, 0, :, :], sv_[:, :, 1, :, :])
                nc.all_engine_barrier()
                src, dsts = dsts, src
            v.tensor_scalar(tob[:, :], src[:, :], 1.0 / float(np.sqrt(8192.0)), 1.0,
                            ALU.mult, ALU.mult)
            nc.all_engine_barrier()
            sy.dma_start(d3[:, :, ts(ct, CT)],
                         tob[:, :].rearrange("p (c k) -> p c k", c=NH)).then_inc(s_o, 16)
            sy.wait_ge(s_o, (ct + 1) * 16)
            nc.all_engine_barrier()
    return p


def build_boundary(mode):
    p = Prog()
    nc = p.nc
    has_y = mode in ("mid", "final")
    has_xn = mode in ("first", "mid")
    x_in = p.inp("x", [T, D], F32)
    y_in = p.inp("y", [D, T], BF16) if has_y else None
    ident_h = p.inp("ident", [P, P], BF16)
    x_out = p.out("xo", [T, D], F32) if mode != "first" else None
    xn_out = p.out("xn", [D, T], BF16) if has_xn else None

    yb = p.sb("yb", [P, D], BF16)
    ytm = p.sb("ytm", [P, D], F32)
    sq = p.sb("sq", [P, D], F32)
    xt = p.sb("xt", [P, D], F32)
    xp = p.sb("xp", [P, D], F32)
    xnb = p.sb("xnb", [P, D], BF16)
    xnf = p.sb("xnf", [P, D], BF16)
    idn = p.sb("idn", [P, P], BF16)
    st1 = p.sb("st1", [P, 1], F32); st2 = p.sb("st2", [P, 1], F32)
    mm_ = p.sb("mm_", [P, 1], F32); vv_ = p.sb("vv_", [P, 1], F32)
    t0_ = p.sb("t0_", [P, 1], F32); rr_ = p.sb("rr_", [P, 1], F32)
    psA = p.ps("psA", [P, 2048], F32)

    s_d = p.sem("s_d")
    s_o = p.sem("s_o")
    nd_in = (1 if has_y else 0) + 1
    n_out = (1 if mode != "first" else 0) + (1 if has_xn else 0)

    with p.es:
        g = nc.gpsimd; te = nc.tensor; s = nc.scalar; v = nc.vector; sy = nc.sync
        g.dma_start(idn[:, :], ident_h[:, :]).then_inc(s_d, 16)
        g.wait_ge(s_d, 16)
        nc.all_engine_barrier()
        with nc.Fori(0, NT, engines=ALLE) as tc:
            if has_y:
                g.dma_start(yb[:, :].rearrange("p (c k) -> p c k", c=ND),
                            y_in.rearrange("(c p) t -> p c t", p=P)[:, :, ts(tc, P)]
                            ).then_inc(s_d, 16)
            g.dma_start(xt[:, :], x_in[ts(tc, P), :]).then_inc(s_d, 16)
            g.wait_ge(s_d, (tc * nd_in + nd_in + 1) * 16)
            nc.all_engine_barrier()
            if has_y:
                for dch in range(ND):
                    te.matmul(psA[:, dch * P:(dch + 1) * P],
                              yb[:, dch * P:(dch + 1) * P], idn[:, :],
                              start=True, stop=True)
                nc.all_engine_barrier()
                s.activation(ytm[:, :], psA[:, :], AF.Copy, accum_out=st1[:, :])
                s.activation(sq[:, :], psA[:, :], AF.Square, accum_out=st2[:, :])
                nc.all_engine_barrier()
                v.tensor_scalar(mm_[:, :], st1[:, :], 1.0 / D, 1.0, ALU.mult, ALU.mult)
                v.tensor_scalar(vv_[:, :], st2[:, :], 1.0 / D, 1.0, ALU.mult, ALU.mult)
                v.tensor_mul(t0_[:, :], mm_[:, :], mm_[:, :])
                v.tensor_sub(vv_[:, :], vv_[:, :], t0_[:, :])
                v.tensor_scalar(vv_[:, :], vv_[:, :], 1.0, EPS, ALU.mult, ALU.add)
                nc.all_engine_barrier()
                s.activation(t0_[:, :], vv_[:, :], AF.Sqrt)
                nc.all_engine_barrier()
                v.reciprocal(rr_[:, :], t0_[:, :])
                v.tensor_scalar(ytm[:, :], ytm[:, :], mm_[:, :], rr_[:, :],
                                ALU.subtract, ALU.mult)
                v.tensor_add(xp[:, :], xt[:, :], ytm[:, :])
                nc.all_engine_barrier()
            else:
                v.tensor_copy(xp[:, :], xt[:, :])
                nc.all_engine_barrier()
            if mode != "first":
                sy.dma_start(x_out[ts(tc, P), :], xp[:, :]).then_inc(s_o, 16)
            if has_xn:
                s.activation(sq[:, :], xp[:, :], AF.Square, accum_out=st2[:, :])
                s.activation(sq[:, :], xp[:, :], AF.Copy, accum_out=st1[:, :])
                nc.all_engine_barrier()
                v.tensor_scalar(mm_[:, :], st1[:, :], 1.0 / D, 1.0, ALU.mult, ALU.mult)
                v.tensor_scalar(vv_[:, :], st2[:, :], 1.0 / D, 1.0, ALU.mult, ALU.mult)
                v.tensor_mul(t0_[:, :], mm_[:, :], mm_[:, :])
                v.tensor_sub(vv_[:, :], vv_[:, :], t0_[:, :])
                v.tensor_scalar(vv_[:, :], vv_[:, :], 1.0, EPS, ALU.mult, ALU.add)
                nc.all_engine_barrier()
                s.activation(t0_[:, :], vv_[:, :], AF.Sqrt)
                nc.all_engine_barrier()
                v.reciprocal(rr_[:, :], t0_[:, :])
                v.tensor_scalar(xnb[:, :], xp[:, :], mm_[:, :], rr_[:, :],
                                ALU.subtract, ALU.mult)
                nc.all_engine_barrier()
                for dch in range(ND):
                    te.matmul(psA[:, dch * P:(dch + 1) * P],
                              xnb[:, dch * P:(dch + 1) * P], idn[:, :],
                              start=True, stop=True)
                nc.all_engine_barrier()
                s.copy(xnf[:, :], psA[:, :])
                nc.all_engine_barrier()
                sy.dma_start(xn_out.rearrange("(c p) t -> p c t", p=P)[:, :, ts(tc, P)],
                             xnf[:, :].rearrange("p (c k) -> p c k", c=ND)).then_inc(s_o, 16)
            sy.wait_ge(s_o, (tc + 1) * n_out * 16)
            nc.all_engine_barrier()
    return p


def build_attn(nq, nkv, alibi):
    p = Prog()
    nc = p.nc
    grp = nq // nkv
    qf = p.inp("qf", [nq * P, T], BF16)
    kf = p.inp("kf", [nkv * P, T], BF16)
    vtm = p.inp("vtm", [nkv, T, P], BF16)
    ident_h = p.inp("ident", [P, P], BF16)
    m0_h = p.inp("m0", [P, P], F32)
    ao = p.out("ao", [nq * P, T], BF16)
    if alibi:
        b0_h = p.inp("b0", [P, T], F32)
        sl_h = p.inp("slopes", [P, 16], F32)

    qb = p.sb("qb", [P, T], BF16)
    kb = p.sb("kb", [P, T], BF16)
    vb = p.sb("vb", [P, NT * P], BF16)
    idn = p.sb("idn", [P, P], BF16)
    m0s = p.sb("m0s", [P, P], F32)
    biasH = p.sb("biasH", [P, T], F32)
    b0s = p.sb("b0s", [P, T], F32)
    sls = p.sb("sls", [P, 16], F32)
    s2 = p.sb("s2", [P, T], F32)
    probs = p.sb("probs", [P, T], F32)
    probs_b = p.sb("probs_b", [P, T], BF16)
    probsT = p.sb("probsT", [P, T], BF16)
    aosb = p.sb("aosb", [P, T], BF16)
    negmax = p.sb("negmax", [P, 1], F32)
    sums = p.sb("sums", [P, 1], F32)
    recip = p.sb("recip", [P, 1], F32)
    diag = p.sb("diag", [P, P], BF16)
    psA = p.ps("psA", [P, 2048], F32)
    psO = p.ps("psO", [P, P], F32)

    s_d = p.sem("s_d")
    s_o = p.sem("s_o")
    nc0 = 2 + (2 if alibi else 0)

    with p.es:
        g = nc.gpsimd; te = nc.tensor; s = nc.scalar; v = nc.vector; sy = nc.sync
        g.dma_start(idn[:, :], ident_h[:, :]).then_inc(s_d, 16)
        g.dma_start(m0s[:, :], m0_h[:, :]).then_inc(s_d, 16)
        if alibi:
            g.dma_start(b0s[:, :], b0_h[:, :]).then_inc(s_d, 16)
            g.dma_start(sls[:, :], sl_h[:, :]).then_inc(s_d, 16)
        g.wait_ge(s_d, 16 * nc0)
        nc.all_engine_barrier()
        if not alibi:
            v.memset(biasH[:, :], 0.0)
            nc.all_engine_barrier()

        def head_body(h, kv):
            g.dma_start(qb[:, :], qf[ts(h, P) if not isinstance(h, int) else
                                     bass.ts(h, P), :]).then_inc(s_d, 16)
            g.dma_start(kb[:, :], kf[ts(kv, P), :]).then_inc(s_d, 16)
            g.dma_start(vb[:, :].rearrange("p (c k) -> p c k", c=NT),
                        vtm.rearrange("a (c p) k -> a p c k", p=P)[kv, :, :, :]
                        ).then_inc(s_d, 16)
            g.wait_ge(s_d, 16 * (nc0 + (h + 1) * 3))
            nc.all_engine_barrier()
            if alibi:
                v.tensor_scalar(biasH[:, :], b0s[:, :], sls[:, ts(h, 1)], 1.0,
                                ALU.mult, ALU.mult)
                nc.all_engine_barrier()
            for qc in range(NT):
                lk = (qc + 1) * P
                nn = (lk + 511) // 512
                for n in range(nn):
                    w_ = min(512, lk - n * 512)
                    te.matmul(psA[:, n * 512:n * 512 + w_],
                              qb[:, qc * P:(qc + 1) * P], kb[:, n * 512:n * 512 + w_],
                              start=True, stop=True)
                nc.all_engine_barrier()
                v.tensor_tensor(s2[:, 0:lk], psA[:, 0:lk], biasH[:, 0:lk], ALU.add)
                v.tensor_tensor(s2[:, lk - P:lk], s2[:, lk - P:lk], m0s[:, :], ALU.add)
                v.reduce_max(negmax[:, :], s2[:, 0:lk], axis=AX.X, negate=True)
                nc.all_engine_barrier()
                s.activation(probs[:, 0:lk], s2[:, 0:lk], AF.Exp,
                             bias=negmax[:, :], accum_out=sums[:, :])
                s.copy(probs_b[:, 0:lk], probs[:, 0:lk])
                nc.all_engine_barrier()
                v.reciprocal(recip[:, :], sums[:, :])
                nc.all_engine_barrier()
                s.activation(diag[:, :], idn[:, :], AF.Copy, scale=recip[:, :])
                nc.all_engine_barrier()
                for kc in range(qc + 1):
                    te.matmul(psA[:, kc * P:(kc + 1) * P],
                              probs_b[:, kc * P:(kc + 1) * P], diag[:, :],
                              start=True, stop=True)
                nc.all_engine_barrier()
                s.copy(probsT[:, 0:lk], psA[:, 0:lk])
                nc.all_engine_barrier()
                for kc in range(qc + 1):
                    te.matmul(psO[:, :], vb[:, kc * P:(kc + 1) * P],
                              probsT[:, kc * P:(kc + 1) * P],
                              start=(kc == 0), stop=(kc == qc))
                nc.all_engine_barrier()
                s.copy(aosb[:, qc * P:(qc + 1) * P], psO[:, :])
                nc.all_engine_barrier()
            sy.dma_start(ao[ts(h, P), :], aosb[:, :]).then_inc(s_o, 16)
            sy.wait_ge(s_o, 16 * (h + 1))
            nc.all_engine_barrier()

        if grp == 1:
            with nc.Fori(0, nq, engines=ALLE) as h:
                head_body(h, h)
        else:
            with nc.Fori(0, nkv, engines=ALLE) as kv:
                with nc.Fori(0, grp, engines=ALLE) as gq:
                    head_body(kv * grp + gq, kv)
    return p


_PROGS = {}


def _get(name, fn):
    if name not in _PROGS:
        _PROGS[name] = fn()
    return _PROGS[name]


def _b2(a):
    n = a.shape[0] // P
    return np.ascontiguousarray(a.reshape(n, P).T).astype(np.float32)


def kernel(**inp):
    C = _consts()
    x0 = np.ascontiguousarray(np.asarray(inp["L0_attn_hidden_states"], np.float32))

    qkv_wT = np.ascontiguousarray(np.asarray(inp["L0_attn_qkv_w"], np.float32).T)
    qkvb2 = _b2(np.asarray(inp["L0_attn_qkv_b"], np.float32))
    vbias0 = np.broadcast_to(np.asarray(inp["L0_attn_qkv_b"], np.float32)[4096:],
                             (P, D)).copy()
    ow0T = np.ascontiguousarray(np.asarray(inp["L0_attn_out_w"], np.float32).T)
    ob0 = _b2(np.asarray(inp["L0_attn_out_b"], np.float32))
    slopes = np.broadcast_to(np.asarray(inp["L0_attn_alibi_slopes"], np.float32),
                             (P, 16)).copy()
    wg0 = np.ascontiguousarray(np.asarray(inp["L0_ffn_w_gate"], np.int8))
    wu0 = np.ascontiguousarray(np.asarray(inp["L0_ffn_w_up"], np.int8))
    wd0 = np.ascontiguousarray(np.asarray(inp["L0_ffn_w_down"], np.int8))
    sg0 = _b2(np.asarray(inp["L0_ffn_s_gate"], np.float32))
    su0 = _b2(np.asarray(inp["L0_ffn_s_up"], np.float32))
    sd0 = _b2(np.asarray(inp["L0_ffn_s_down"], np.float32))
    wq1T = np.ascontiguousarray(np.asarray(inp["L1_attn_wq"], np.float32).T)
    wk1T = np.ascontiguousarray(np.asarray(inp["L1_attn_wk"], np.float32).T)
    wv1T = np.ascontiguousarray(np.asarray(inp["L1_attn_wv"], np.float32).T)
    wo1T = np.ascontiguousarray(np.asarray(inp["L1_attn_wo"], np.float32).T)
    bq1 = _b2(np.asarray(inp["L1_attn_bq"], np.float32))
    bk1 = _b2(np.asarray(inp["L1_attn_bk"], np.float32))
    vbias1 = np.broadcast_to(np.asarray(inp["L1_attn_bv"], np.float32),
                             (P, 256)).copy()
    W1T = np.ascontiguousarray(np.asarray(inp["L1_ffn_W1"], np.float32).T)
    W3T = np.ascontiguousarray(np.asarray(inp["L1_ffn_W3"], np.float32).T)
    W2T = np.ascontiguousarray(np.asarray(inp["L1_ffn_W2"], np.float32).T)

    cores = list(range(NCORES))
    idn = np.asarray(C["ident"])

    def run(name, fn, per_core):
        p = _get(name, fn)
        res = run_bass_kernel_spmd(p.nc, [per_core(b) for b in cores],
                                   core_ids=cores)
        return res.results

    r = run("w2prep", build_w2prep,
            lambda b: {"w2t": W2T, "h128": np.asarray(C["h128"])})
    W2PT = [x["w2pt"] for x in r]

    r = run("bfirst", lambda: build_boundary("first"),
            lambda b: {"x": x0[b], "ident": idn})
    xn = [x["xn"] for x in r]
    xres = [x0[b] for b in cores]

    def attn_block(lyr, xn):
        if lyr == 0:
            r = run("q0", lambda: build_mm_phase(D, 16, F32, "bias_scale",
                                                 has_bias=True, scale_imm=SC),
                    lambda b: {"xn": xn[b], "w": qkv_wT[:, :2048],
                               "bias": qkvb2[:, 0:16]})
            qfv = [x["o"] for x in r]
            r = run("k0", lambda: build_mm_phase(D, 16, F32, "bias_scale",
                                                 has_bias=True),
                    lambda b: {"xn": xn[b], "w": qkv_wT[:, 2048:4096],
                               "bias": qkvb2[:, 16:32]})
            kfv = [x["o"] for x in r]
            r = run("vtm0", lambda: build_vtm(2048, 16),
                    lambda b: {"xn": xn[b], "w": qkv_wT[:, 4096:6144],
                               "vbias": vbias0})
            vt = [x["vtm"] for x in r]
            r = run("attn0", lambda: build_attn(16, 16, True),
                    lambda b: {"qf": qfv[b], "kf": kfv[b], "vtm": vt[b],
                               "ident": idn, "m0": C["m0"], "b0": C["b0"],
                               "slopes": slopes})
            aov = [x["ao"] for x in r]
            r = run("oproj0", lambda: build_mm_phase(D, 16, F32, "bias_scale",
                                                     has_bias=True),
                    lambda b: {"xn": aov[b], "w": ow0T, "bias": ob0})
            return [x["o"] for x in r]
        else:
            r = run("q1", lambda: build_mm_phase(D, 16, F32, "rope",
                                                 has_bias=True),
                    lambda b: {"xn": xn[b], "w": wq1T, "bias": bq1,
                               "cc": C["ccq"], "ss": C["ssq"]})
            qfv = [x["o"] for x in r]
            r = run("k1", lambda: build_mm_phase(D, 2, F32, "rope",
                                                 has_bias=True),
                    lambda b: {"xn": xn[b], "w": wk1T, "bias": bk1,
                               "cc": C["cc"], "ss": C["ss"]})
            kfv = [x["o"] for x in r]
            r = run("vtm1", lambda: build_vtm(256, 2),
                    lambda b: {"xn": xn[b], "w": wv1T, "vbias": vbias1})
            vt = [x["vtm"] for x in r]
            r = run("attn1", lambda: build_attn(16, 2, False),
                    lambda b: {"qf": qfv[b], "kf": kfv[b], "vtm": vt[b],
                               "ident": idn, "m0": C["m0"]})
            aov = [x["ao"] for x in r]
            r = run("oproj1", lambda: build_mm_phase(D, 16, F32, "plain"),
                    lambda b: {"xn": aov[b], "w": wo1T})
            return [x["o"] for x in r]

    def ffn_block(lyr, xn):
        if lyr == 0:
            r = run("gu0", lambda: build_gateup(I8, True),
                    lambda b: {"xn": xn[b], "wg": wg0, "wu": wu0,
                               "sg": sg0, "su": su0})
            fu = [x["fuse"] for x in r]
            r = run("dn0", lambda: build_down(I8, True),
                    lambda b: {"fuse": fu[b], "wd": wd0, "sd": sd0})
            return [x["y"] for x in r]
        else:
            r = run("gu1", lambda: build_gateup(F32, False),
                    lambda b: {"xn": xn[b], "wg": W1T, "wu": W3T})
            fu = [x["fuse"] for x in r]
            r = run("dn1", lambda: build_down(BF16, False),
                    lambda b: {"fuse": fu[b], "wd": W2PT[b]})
            return [x["y"] for x in r]

    def boundary(xres, y, mode):
        r = run("b_" + mode, lambda: build_boundary(mode),
                lambda b: {"x": xres[b], "y": y[b], "ident": idn})
        if mode == "final":
            return [x["xo"] for x in r], None
        return [x["xo"] for x in r], [x["xn"] for x in r]

    y = attn_block(0, xn)
    xres, xn = boundary(xres, y, "mid")
    y = ffn_block(0, xn)
    xres, xn = boundary(xres, y, "mid")
    y = attn_block(1, xn)
    xres, xn = boundary(xres, y, "mid")
    y = ffn_block(1, xn)
    xout, _ = boundary(xres, y, "final")

    return np.stack(xout, axis=0).astype(np.float32)


# version 05 (final)
# speedup vs baseline: 1.1686x; 1.1686x over previous
import numpy as np
import ml_dtypes
from contextlib import ExitStack

import concourse.bass as bass
import concourse.mybir as mybir
from concourse.bass_utils import run_bass_kernel_spmd

F32 = mybir.dt.float32
BF16 = mybir.dt.bfloat16
I8 = mybir.dt.int8
AF = mybir.ActivationFunctionType
ALU = mybir.AluOpType
AX = mybir.AxisListType
ts = bass.ts
bfdt = ml_dtypes.bfloat16

P = 128
T = 2048
D = 2048
HID = 8192
NT = T // P
ND = D // P
EPS = 1e-5
SC = 1.0 / np.sqrt(128.0)
NCORES = 8
ALLE = mybir.ALL_ENGINES


def _had(n):
    h = np.array([[1.0]])
    while h.shape[0] < n:
        h = np.block([[h, h], [h, -h]])
    return h


def _consts():
    c = {}
    c["ident"] = np.eye(P, dtype=bfdt)
    c["m0"] = np.where(np.arange(P)[None, :] <= np.arange(P)[:, None], 0.0,
                       -1.0e9).astype(np.float32)
    c["b0"] = np.broadcast_to(np.arange(T, dtype=np.float32), (P, T)).copy()
    c["h128"] = _had(128).astype(bfdt)
    half = 64
    inv = 10000.0 ** (-np.arange(half, dtype=np.float64) / half)
    ang = inv[:, None] * np.arange(T, dtype=np.float64)[None, :]
    cc = np.concatenate([np.cos(ang), np.cos(ang)], 0)
    ss = np.concatenate([-np.sin(ang), np.sin(ang)], 0)
    c["cc"] = cc.astype(np.float32)
    c["ss"] = ss.astype(np.float32)
    c["ccq"] = (cc * SC).astype(np.float32)
    c["ssq"] = (ss * SC).astype(np.float32)
    return c


class Prog:
    def __init__(self):
        self.nc = bass.Bass("TRN2", target_bir_lowering=False,
                            detect_race_conditions=False)
        self.es = ExitStack()
        self.cur = self.es
        self.tag = ""
        self.rbs = {}

    def begin(self, tag):
        self.tag = tag
        self.cur = ExitStack()

    def end(self):
        self.cur.close()
        self.cur = self.es
        self.tag = ""

    def inp(self, name, shape, dt=F32):
        return self.nc.dram_tensor(name, list(shape), dt, kind="ExternalInput")

    def out(self, name, shape, dt=F32):
        return self.nc.dram_tensor(name, list(shape), dt, kind="ExternalOutput")

    def intern(self, name, shape, dt=F32):
        return self.nc.dram_tensor(name, list(shape), dt, kind="Internal")

    def sb(self, name, shape, dt):
        return self.cur.enter_context(self.nc.sbuf_tensor(self.tag + name, list(shape), dt))

    def ps(self, name, shape, dt=F32):
        return self.cur.enter_context(self.nc.psum_tensor(self.tag + name, list(shape), dt))

    def sem(self, name):
        return self.nc.alloc_semaphore(self.tag + name)



from concourse.expressions import RuntimeValue
from concourse.bass_primitives import RegisterHandle
from contextlib import contextmanager


class _RB:

    def __init__(self, eng):
        self.eng = eng
        self.loops = [eng.alloc_register(f"mfl{i}") for i in range(2)]
        self.tmps = [eng.alloc_register(f"mft{i}") for i in range(6)]
        self.ti = 0

    def tmp(self):
        r = self.tmps[self.ti % 6]
        self.ti += 1
        return r


def _rb(p, eng):
    key = eng.engine
    if key not in p.rbs:
        p.rbs[key] = _RB(eng)
    return p.rbs[key]


def wval(p, eng, terms, const, vmin=None, vmax=None):
    rb = _rb(p, eng)
    r = rb.tmp()
    lo = hi = const
    if not terms:
        return const
    (rv0, c0) = terms[0]
    eng.reg_alu(r, rv0, c0, ALU.mult)
    lo += (rv0.min_val or 0) * c0
    hi += (rv0.max_val or 0) * c0
    for rv, c in terms[1:]:
        rb2 = _rb(p, eng)
        t2 = rb2.tmp()
        eng.reg_alu(t2, rv, c, ALU.mult)
        eng.reg_alu(r, r, t2, ALU.add)
        lo += (rv.min_val or 0) * c
        hi += (rv.max_val or 0) * c
    if const:
        eng.reg_alu(r, r, const, ALU.add)
    return RuntimeValue(r, min_val=vmin if vmin is not None else lo,
                        max_val=vmax if vmax is not None else hi)


@contextmanager
def myfori(p, engines, start, end, lvl=0):
    nc = p.nc
    engs = [nc.engines[e] if not hasattr(e, "engine") else e for e in engines]
    regs = [_rb(p, e).loops[lvl] for e in engs]
    loop_id = nc.next_id()
    loop_start = f"mf_{loop_id}_loop"
    loop_end = f"mf_{loop_id}_end"
    etypes = [e.engine for e in engs]
    for e, r in zip(engs, regs):
        e.reg_mov(r, start)
    nc.br(loop_start, engines=etypes)
    from concourse.bass_primitives import RegisterHandles as RHs
    handles = RHs(regs)
    with nc.body(loop_start, valid_engines=etypes):
        yield RuntimeValue(handles, min_val=start, max_val=end - 1)
        for e, r in zip(engs, regs):
            e.reg_alu(r, r, 1, ALU.add)
        nc.br_cmp(handles, end, on_true=loop_start, on_false=loop_end,
                  comp_op="IS_LT", engines=etypes)
    nc.switch_bb(loop_end)


def emit_mm(p, xn, w, dst, bias_h, scol_h, cc_h, ss_h,
            din, dout_chunks, epi, tiles=1, scale_imm=None, act_fn="copy"):
    rope = (epi == "rope")
    has_bias = bias_h is not None
    act_scale = scol_h is not None
    nc = p.nc
    nki = din // P
    TT_ = T // tiles
    w3 = w.rearrange("(i p) o -> p i o", p=P)

    xnb = p.sb("xnb", [P, nki * TT_], BF16)
    wb = [p.sb(f"wb{i}", [P, nki * P], BF16) for i in range(2)]
    tmpf = [p.sb(f"tmpf{i}", [P, TT_], F32) for i in range(2)]
    ob = [p.sb(f"ob{i}", [P, TT_], BF16) for i in range(2)]
    psb = [p.ps(f"psb{i}", [P, min(TT_, 2048)], F32) for i in range(2)]
    bias_sb = p.sb("bias_sb", [P, dout_chunks], F32)
    scol_sb = p.sb("scol_sb", [P, dout_chunks], F32)
    if rope:
        cc_sb = p.sb("cc_sb", [P, T], F32)
        ss_sb = p.sb("ss_sb", [P, T], F32)
        rtmp = p.sb("rtmp", [P, T], F32)
        rtm2 = p.sb("rtm2", [P, T], F32)

    s_x = p.sem("s_x")
    s_w = [p.sem(f"s_w{i}") for i in range(2)]
    s_mm = p.sem("s_mm")
    s_se = p.sem("s_se")
    s_g = p.sem("s_g")
    s_st = p.sem("s_st")

    jobs = [(tt, o) for tt in range(tiles) for o in range(dout_chunks)]
    njob = len(jobs)
    nconst = (1 if has_bias else 0) + (1 if act_scale else 0) + (2 if rope else 0)
    g_is_epi = epi in ("bias_scale", "rope")

    with nc.Block() as block:
        @block.gpsimd
        def _(g_):
            if has_bias:
                g_.dma_start(bias_sb[:, :], bias_h[:, :]).then_inc(s_x, 16)
            if act_scale:
                g_.dma_start(scol_sb[:, :], scol_h[:, :]).then_inc(s_x, 16)
            if rope:
                g_.dma_start(cc_sb[:, :], cc_h[:, :]).then_inc(s_x, 16)
                g_.dma_start(ss_sb[:, :], ss_h[:, :]).then_inc(s_x, 16)
            nx = 0
            for k, (tt, o) in enumerate(jobs):
                par = k % 2
                if o == 0:
                    if k > 0:
                        g_.wait_ge(s_se, k)
                    g_.dma_start(
                        xnb[:, :].rearrange("p (i t) -> p i t", i=nki),
                        xn.rearrange("(i p) t -> p i t", p=P)[:, :, ts(tt, TT_)]
                    ).then_inc(s_x, 16)
                    nx += 1
                if k >= 2:
                    g_.wait_ge(s_mm, k - 1)
                g_.dma_start(wb[par][:, :].rearrange("p (i c) -> p i c", i=nki),
                             w3[:, :, ts(o, P)]).then_inc(s_w[par], 16)
                if g_is_epi:
                    g_.wait_ge(s_se, k + 1)
                    if k >= 2:
                        g_.wait_ge(s_st, 16 * (k - 1))
                    src = tmpf[par]
                    if epi == "bias_scale":
                        if has_bias:
                            ge = g_.tensor_scalar(ob[par][:, :], src[:, :],
                                                  bias_sb[:, o:o + 1],
                                                  scale_imm or 1.0, ALU.add, ALU.mult)
                        else:
                            ge = g_.tensor_scalar(ob[par][:, :], src[:, :],
                                                  scale_imm or 1.0, 1.0,
                                                  ALU.mult, ALU.mult)
                    else:
                        g_.tensor_scalar(rtmp[:, :], src[:, :],
                                         bias_sb[:, o:o + 1], 1.0, ALU.add, ALU.mult)
                        g_.tensor_copy(rtm2[0:64, :], rtmp[64:128, :])
                        g_.tensor_copy(rtm2[64:128, :], rtmp[0:64, :])
                        g_.tensor_mul(rtmp[:, :], rtmp[:, :], cc_sb[:, :])
                        g_.tensor_mul(rtm2[:, :], rtm2[:, :], ss_sb[:, :])
                        ge = g_.tensor_add(ob[par][:, :], rtmp[:, :], rtm2[:, :])
                    ge.then_inc(s_g, 1)

        @block.tensor
        def _(t_):
            nx = 0
            for k, (tt, o) in enumerate(jobs):
                par = k % 2
                if o == 0:
                    nx += 1
                t_.wait_ge(s_x, 16 * (nconst + nx))
                t_.wait_ge(s_w[par], 16 * (k // 2 + 1))
                if k >= 2:
                    t_.wait_ge(s_se, k - 1)
                nn = (TT_ + 511) // 512
                for n in range(nn):
                    w_ = min(512, TT_ - n * 512)
                    for i in range(nki):
                        mm = t_.matmul(psb[par][:, n * 512:n * 512 + w_],
                                       wb[par][:, i * P:(i + 1) * P],
                                       xnb[:, i * TT_ + n * 512: i * TT_ + n * 512 + w_],
                                       start=(i == 0), stop=(i == nki - 1))
                mm.then_inc(s_mm, 1)

        @block.scalar
        def _(s_):
            for k, (tt, o) in enumerate(jobs):
                par = k % 2
                s_.wait_ge(s_mm, k + 1)
                if g_is_epi:
                    if k >= 2:
                        s_.wait_ge(s_g, k - 1)
                    s_.copy(tmpf[par][:, :], psb[par][:, :]).then_inc(s_se, 1)
                else:
                    if k >= 2:
                        s_.wait_ge(s_st, 16 * (k - 1))
                    if act_scale:
                        fn = AF.Silu if act_fn == "silu" else AF.Copy
                        s_.activation(ob[par][:, :], psb[par][:, :], fn,
                                      scale=scol_sb[:, o:o + 1]).then_inc(s_se, 1)
                    else:
                        s_.copy(ob[par][:, :], psb[par][:, :]).then_inc(s_se, 1)

        @block.sync
        def _(sy):
            for k, (tt, o) in enumerate(jobs):
                par = k % 2
                sy.wait_ge(s_g if g_is_epi else s_se, k + 1)
                if tiles == 1:
                    sy.dma_start(dst[ts(o, P), :], ob[par][:, :]).then_inc(s_st, 16)
                else:
                    sy.dma_start(dst[ts(o, P), ts(tt, TT_)],
                                 ob[par][:, :]).then_inc(s_st, 16)
            sy.wait_ge(s_st, 16 * njob)


def build_mm_phase(din, dout_chunks, wdt, epi, tiles=1, has_bias=False,
                   scale_imm=None, act_scale=False, act_fn="copy"):
    rope = (epi == "rope")
    p = Prog()
    xn = p.inp("xn", [din, T], BF16)
    w = p.inp("w", [din, dout_chunks * P], wdt)
    dst = p.out("o", [dout_chunks * P, T], BF16)
    bias_h = p.inp("bias", [P, dout_chunks], F32) if has_bias else None
    scol_h = p.inp("scol", [P, dout_chunks], F32) if act_scale else None
    cc_h = p.inp("cc", [P, T], F32) if rope else None
    ss_h = p.inp("ss", [P, T], F32) if rope else None
    emit_mm(p, xn, w, dst, bias_h, scol_h, cc_h, ss_h, din, dout_chunks, epi,
            tiles, scale_imm, act_fn)
    return p


def emit_vtm(p, xn, w, vb, dst, dout, nkvh):
    nc = p.nc
    nki = ND
    w3 = w.rearrange("(i p) o -> p i o", p=P)

    xnb = p.sb("xnb", [P, nki * T], BF16)
    wball = p.sb("wball", [P, nki * dout], BF16)
    vb_sb = p.sb("vb_sb", [P, dout], F32)
    tmpf = [p.sb(f"tmpf{i}", [P, dout], F32) for i in range(2)]
    ob = [p.sb(f"ob{i}", [P, dout], BF16) for i in range(2)]
    nn = (dout + 511) // 512
    psb = [p.ps(f"psb{i}", [P, min(dout, 2048)], F32) for i in range(2)]

    s_x = p.sem("s_x"); s_mm = p.sem("s_mm"); s_se = p.sem("s_se")
    s_g = p.sem("s_g"); s_st = p.sem("s_st")

    with nc.Block() as block:
        @block.gpsimd
        def _(g_):
            g_.dma_start(vb_sb[:, :], vb[:, :]).then_inc(s_x, 16)
            g_.dma_start(xnb[:, :].rearrange("p (i t) -> p i t", i=nki),
                         xn.rearrange("(i p) t -> p i t", p=P)).then_inc(s_x, 16)
            g_.dma_start(wball[:, :].rearrange("p (i c) -> p i c", i=nki),
                         w3[:, :, :]).then_inc(s_x, 16)
            for k in range(NT):
                par = k % 2
                g_.wait_ge(s_se, k + 1)
                if k >= 2:
                    g_.wait_ge(s_st, 16 * nkvh * (k - 1))
                g_.tensor_add(ob[par][:, :], tmpf[par][:, :],
                              vb_sb[:, :]).then_inc(s_g, 1)

        @block.tensor
        def _(t_):
            t_.wait_ge(s_x, 48)
            for k in range(NT):
                par = k % 2
                if k >= 2:
                    t_.wait_ge(s_se, k - 1)
                for n in range(nn):
                    w_ = min(512, dout - n * 512)
                    for i in range(nki):
                        mm = t_.matmul(psb[par][:, n * 512:n * 512 + w_],
                                       xnb[:, i * T + k * P: i * T + (k + 1) * P],
                                       wball[:, i * dout + n * 512: i * dout + n * 512 + w_],
                                       start=(i == 0), stop=(i == nki - 1))
                mm.then_inc(s_mm, 1)

        @block.scalar
        def _(s_):
            for k in range(NT):
                par = k % 2
                s_.wait_ge(s_mm, k + 1)
                if k >= 2:
                    s_.wait_ge(s_g, k - 1)
                s_.copy(tmpf[par][:, :], psb[par][:, :]).then_inc(s_se, 1)

        @block.sync
        def _(sy):
            for k in range(NT):
                par = k % 2
                sy.wait_ge(s_g, k + 1)
                for h in range(nkvh):
                    sy.dma_start(dst[h, ts(k, P), :],
                                 ob[par][:, h * P:(h + 1) * P]).then_inc(s_st, 16)
            sy.wait_ge(s_st, 16 * nkvh * NT)


def build_vtm(dout, nkvh):
    p = Prog()
    xn = p.inp("xn", [D, T], BF16)
    w = p.inp("w", [D, dout], F32)
    vb = p.inp("vbias", [P, dout], F32)
    dst = p.out("vtm", [nkvh, T, P], BF16)
    emit_vtm(p, xn, w, vb, dst, dout, nkvh)
    return p


def emit_gateup(p, xn, wg, wu, dst, sg_h, su_h):
    with_scales = sg_h is not None
    nc = p.nc
    nki = ND
    NH = HID // P
    wg3 = wg.rearrange("(i p) o -> p i o", p=P)
    wu3 = wu.rearrange("(i p) o -> p i o", p=P)

    xnb = p.sb("xnb", [P, nki * T], BF16)
    wgb = [p.sb(f"wgb{i}", [P, nki * P], BF16) for i in range(2)]
    wub = [p.sb(f"wub{i}", [P, nki * P], BF16) for i in range(2)]
    gsb = [p.sb(f"gsb{i}", [P, 1024], F32) for i in range(2)]
    usb = [p.sb(f"usb{i}", [P, 1024], F32) for i in range(2)]
    fsb = [p.sb(f"fsb{i}", [P, T], BF16) for i in range(2)]
    sg_sb = p.sb("sg_sb", [P, NH], F32)
    su_sb = p.sb("su_sb", [P, NH], F32)
    psg = [p.ps(f"psg{i}", [P, 1024], F32) for i in range(2)]
    psu = [p.ps(f"psu{i}", [P, 1024], F32) for i in range(2)]

    s_x = p.sem("s_x"); s_w = [p.sem(f"s_w{i}") for i in range(2)]
    s_mm = p.sem("s_mm"); s_se = p.sem("s_se"); s_ve = p.sem("s_ve")
    s_st = p.sem("s_st")

    nconst = 2 if with_scales else 0

    with p.es, nc.Block() as block:
        @block.gpsimd
        def _(g_):
            if with_scales:
                g_.dma_start(sg_sb[:, :], sg_h[:, :]).then_inc(s_x, 16)
                g_.dma_start(su_sb[:, :], su_h[:, :]).then_inc(s_x, 16)
            g_.dma_start(xnb[:, :].rearrange("p (i t) -> p i t", i=nki),
                         xn.rearrange("(i p) t -> p i t", p=P)).then_inc(s_x, 16)

            def body(ch, first):
                for cpar in range(2):
                    c = ch * 2 + cpar
                    if not first:
                        g_.wait_ge(s_mm, wval(p, g_, [(ch, 4)], -2 + 2 * cpar))
                    g_.dma_start(wgb[cpar][:, :].rearrange("p (i c) -> p i c", i=nki),
                                 wg3[:, :, ts(c, P)]).then_inc(s_w[cpar], 16)
                    g_.dma_start(wub[cpar][:, :].rearrange("p (i c) -> p i c", i=nki),
                                 wu3[:, :, ts(c, P)]).then_inc(s_w[cpar], 16)
                for cpar in range(2):
                    for hf in range(2):
                        if first:
                            g_.wait_ge(s_se, 2 * (2 * cpar + hf + 1))
                        else:
                            g_.wait_ge(s_se, wval(p, g_, [(ch, 8)],
                                                  2 * (2 * cpar + hf + 1)))
                            if hf == 0:
                                g_.wait_ge(s_st, wval(p, g_, [(ch, 32)],
                                                      16 * (cpar - 1)))
                        g_.tensor_mul(fsb[cpar][:, hf * 1024:(hf + 1) * 1024],
                                      gsb[hf][:, :], usb[hf][:, :]).then_inc(s_ve, 1)
            body(0, True)
            with myfori(p, [g_], 1, NH // 2) as ch:
                body(ch, False)

        @block.tensor
        def _(t_):
            t_.wait_ge(s_x, 16 * (nconst + 1))

            def job(ch, cpar, hf, first):
                pp = hf
                if isinstance(ch, int):
                    t_.wait_ge(s_w[cpar], 32 * (ch + 1))
                    if not first:
                        t_.wait_ge(s_se, 2 * (4 * ch + 2 * cpar + hf - 1))
                else:
                    t_.wait_ge(s_w[cpar], wval(p, t_, [(ch, 32)], 32))
                    t_.wait_ge(s_se, wval(p, t_, [(ch, 8)],
                                          2 * (2 * cpar + hf - 1)))
                for n in range(2):
                    off = hf * 1024 + n * 512
                    for i in range(nki):
                        t_.matmul(psg[pp][:, n * 512:(n + 1) * 512],
                                  wgb[cpar][:, i * P:(i + 1) * P],
                                  xnb[:, i * T + off: i * T + off + 512],
                                  start=(i == 0), stop=(i == nki - 1))
                for n in range(2):
                    off = hf * 1024 + n * 512
                    for i in range(nki):
                        mm = t_.matmul(psu[pp][:, n * 512:(n + 1) * 512],
                                       wub[cpar][:, i * P:(i + 1) * P],
                                       xnb[:, i * T + off: i * T + off + 512],
                                       start=(i == 0), stop=(i == nki - 1))
                mm.then_inc(s_mm, 1)
            for cpar in range(2):
                for hf in range(2):
                    job(0, cpar, hf, 2 * cpar + hf < 2)
            with myfori(p, [t_], 1, NH // 2) as ch:
                for cpar in range(2):
                    for hf in range(2):
                        job(ch, cpar, hf, False)

        @block.scalar
        def _(s_):
            for c in range(NH):
                for hf in range(2):
                    k = 2 * c + hf
                    pp = hf
                    s_.wait_ge(s_mm, k + 1)
                    if k >= 2:
                        s_.wait_ge(s_ve, k - 1)
                    if with_scales:
                        s_.activation(gsb[pp][:, :], psg[pp][:, :], AF.Silu,
                                      scale=sg_sb[:, c:c + 1])
                        s_.activation(usb[pp][:, :], psu[pp][:, :], AF.Copy,
                                      scale=su_sb[:, c:c + 1]).then_inc(s_se, 2)
                    else:
                        s_.activation(gsb[pp][:, :], psg[pp][:, :], AF.Silu)
                        s_.copy(usb[pp][:, :], psu[pp][:, :]).then_inc(s_se, 2)
                if c >= 1:
                    s_.wait_ge(s_ve, 2 * c)
                    s_.dma_start(dst[(c - 1) * P:c * P, :],
                                 fsb[(c - 1) % 2][:, :]).then_inc(s_st, 16)
            s_.wait_ge(s_ve, 2 * NH)
            s_.dma_start(dst[(NH - 1) * P:NH * P, :],
                         fsb[(NH - 1) % 2][:, :]).then_inc(s_st, 16)
            s_.wait_ge(s_st, 16 * NH)


def build_gateup(wdt, with_scales):
    p = Prog()
    xn = p.inp("xn", [D, T], BF16)
    wg = p.inp("wg", [D, HID], wdt)
    wu = p.inp("wu", [D, HID], wdt)
    dst = p.out("fuse", [HID, T], BF16)
    NH = HID // P
    sg_h = p.inp("sg", [P, NH], F32) if with_scales else None
    su_h = p.inp("su", [P, NH], F32) if with_scales else None
    emit_gateup(p, xn, wg, wu, dst, sg_h, su_h)
    return p


def emit_down(p, fuse, wd, dst, sd_h):
    with_scale = sd_h is not None
    nc = p.nc
    NH = HID // P
    TT_ = 512
    tiles = T // TT_
    wd3 = wd.rearrange("(i p) o -> p i o", p=P)

    fub = p.sb("fub", [P, NH * TT_], BF16)
    wdb = [p.sb(f"wdb{i}", [P, NH * P], BF16) for i in range(2)]
    ob = [p.sb(f"ob{i}", [P, TT_], BF16) for i in range(2)]
    sd_sb = p.sb("sd_sb", [P, ND], F32)
    psb = [p.ps(f"psb{i}", [P, TT_], F32) for i in range(2)]

    s_x = p.sem("s_x"); s_w = [p.sem(f"s_w{i}") for i in range(2)]
    s_mm = p.sem("s_mm"); s_se = p.sem("s_se"); s_st = p.sem("s_st")
    s_so = p.sem("s_so")

    nconst = 1 if with_scale else 0
    NOP = ND // 2

    with p.es, nc.Block() as block:
        @block.gpsimd
        def _(g_):
            if with_scale:
                g_.dma_start(sd_sb[:, :], sd_h[:, :]).then_inc(s_x, 16)

            def fload(tt):
                g_.dma_start(fub[:, :].rearrange("p (i t) -> p i t", i=NH),
                             fuse.rearrange("(i p) t -> p i t", p=P)[:, :, ts(tt, TT_)]
                             ).then_inc(s_x, 16)

            def body(tt, op, first):
                for par in range(2):
                    if not first or par == 1:
                        if isinstance(tt, int) and isinstance(op, int):
                            g_.wait_ge(s_mm, tt * ND + op * 2 + par - 1)
                        else:
                            tm = [(tt, ND)] if not isinstance(tt, int) else []
                            tm += [(op, 2)] if not isinstance(op, int) else []
                            cst = (tt * ND if isinstance(tt, int) else 0) + \
                                  (op * 2 if isinstance(op, int) else 0) + par - 1
                            g_.wait_ge(s_mm, wval(p, g_, tm, cst))
                    g_.dma_start(wdb[par][:, :].rearrange("p (i c) -> p i c", i=NH),
                                 wd3[:, :, ts(op * 2 + par, P)]).then_inc(s_w[par], 16)
            fload(0)
            body(0, 0, True)
            with myfori(p, [g_], 1, NOP, 1) as op:
                body(0, op, False)
            with myfori(p, [g_], 1, tiles, 0) as tt:
                g_.wait_ge(s_se, wval(p, g_, [(tt, ND)], 0))
                fload(tt)
                with myfori(p, [g_], 0, NOP, 1) as op:
                    body(tt, op, False)

        @block.tensor
        def _(t_):
            def body(tt, op, first):
                for par in range(2):
                    if isinstance(tt, int) and isinstance(op, int):
                        t_.wait_ge(s_w[par], 16 * (tt * NOP + op + 1))
                        if not first:
                            t_.wait_ge(s_se, tt * ND + op * 2 + par - 1)
                    else:
                        tm = [(tt, 16 * NOP)] if not isinstance(tt, int) else []
                        tm += [(op, 16)] if not isinstance(op, int) else []
                        cst = (16 * tt * NOP if isinstance(tt, int) else 0) + \
                              (16 * op if isinstance(op, int) else 0) + 16
                        t_.wait_ge(s_w[par], wval(p, t_, tm, cst))
                        tm2 = [(tt, ND)] if not isinstance(tt, int) else []
                        tm2 += [(op, 2)] if not isinstance(op, int) else []
                        cst2 = (tt * ND if isinstance(tt, int) else 0) + \
                               (op * 2 if isinstance(op, int) else 0) + par - 1
                        t_.wait_ge(s_se, wval(p, t_, tm2, cst2))
                    for i in range(NH):
                        mm = t_.matmul(psb[par][:, :],
                                       wdb[par][:, i * P:(i + 1) * P],
                                       fub[:, i * TT_:(i + 1) * TT_],
                                       start=(i == 0), stop=(i == NH - 1))
                    mm.then_inc(s_mm, 1)
            t_.wait_ge(s_x, 16 * (nconst + 1))
            body(0, 0, True)
            with myfori(p, [t_], 1, NOP, 1) as op:
                body(0, op, False)
            with myfori(p, [t_], 1, tiles, 0) as tt:
                t_.wait_ge(s_x, wval(p, t_, [(tt, 16)], 16 * (nconst + 1)))
                with myfori(p, [t_], 0, NOP, 1) as op:
                    body(tt, op, False)

        @block.scalar
        def _(s_):
            for tt in range(tiles):
                for o in range(ND):
                    par = o % 2
                    k = tt * ND + o
                    s_.wait_ge(s_mm, k + 1)
                    if k >= 2:
                        s_.wait_ge(s_so, 16 * (k - 1))
                    if with_scale:
                        s_.activation(ob[par][:, :], psb[par][:, :], AF.Copy,
                                      scale=sd_sb[:, o:o + 1]).then_inc(s_se, 1)
                    else:
                        s_.copy(ob[par][:, :], psb[par][:, :]).then_inc(s_se, 1)
                    s_.wait_ge(s_se, k + 1)
                    s_.dma_start(dst[o * P:(o + 1) * P, tt * 512:(tt + 1) * 512],
                                 ob[par][:, :]).then_inc(s_so, 16)
            s_.wait_ge(s_so, 16 * tiles * ND)


def build_down(wdt, with_scale):
    p = Prog()
    fuse = p.inp("fuse", [HID, T], BF16)
    wd = p.inp("wd", [HID, D], wdt)
    dst = p.out("y", [D, T], BF16)
    sd_h = p.inp("sd", [P, ND], F32) if with_scale else None
    emit_down(p, fuse, wd, dst, sd_h)
    return p


def emit_w2prep(p, w2t, h128, dst):
    nc = p.nc
    CT = 256
    nct = D // CT
    NH = HID // P
    w3 = w2t.rearrange("(c p) o -> p c o", p=P)
    d3 = dst.rearrange("(c p) o -> p c o", p=P)

    win = p.sb("win", [P, NH * CT], BF16)
    ta = p.sb("ta", [P, NH * CT], F32)
    tb = p.sb("tb", [P, NH * CT], F32)
    tob = p.sb("tob", [P, NH * CT], BF16)
    h_sb = p.sb("h_sb", [P, P], BF16)
    psq = p.ps("psq", [P, 2048], F32)

    s_d = p.sem("s_d")
    s_o = p.sem("s_o")

    if True:
        g = nc.gpsimd; te = nc.tensor; s = nc.scalar; sy = nc.sync
        v = nc.vector
        g.dma_start(h_sb[:, :], h128[:, :]).then_inc(s_d, 16)
        g.wait_ge(s_d, 16)
        nc.all_engine_barrier()
        with myfori(p, ALLE, 0, nct) as ct:
            g.dma_start(win[:, :].rearrange("p (c k) -> p c k", c=NH),
                        w3[:, :, ts(ct, CT)]).then_inc(s_d, 16)
            g.wait_ge(s_d, wval(p, g, [(ct, 16)], 32))
            nc.all_engine_barrier()
            for rnd in range(NH // 8):
                for j in range(8):
                    c = rnd * 8 + j
                    te.matmul(psq[:, j * CT:(j + 1) * CT], h_sb[:, :],
                              win[:, c * CT:(c + 1) * CT], start=True, stop=True)
                nc.all_engine_barrier()
                for j in range(8):
                    c = rnd * 8 + j
                    s.copy(ta[:, c * CT:(c + 1) * CT], psq[:, j * CT:(j + 1) * CT])
                nc.all_engine_barrier()
            src, dsts = ta, tb
            for st in range(6):
                hc = 1 << st
                sv_ = src[:, :].rearrange("p (g two j k) -> p g two j k",
                                          two=2, j=hc, k=CT)
                dv_ = dsts[:, :].rearrange("p (g two j k) -> p g two j k",
                                           two=2, j=hc, k=CT)
                v.tensor_add(dv_[:, :, 0, :, :], sv_[:, :, 0, :, :], sv_[:, :, 1, :, :])
                v.tensor_sub(dv_[:, :, 1, :, :], sv_[:, :, 0, :, :], sv_[:, :, 1, :, :])
                nc.all_engine_barrier()
                src, dsts = dsts, src
            v.tensor_scalar(tob[:, :], src[:, :], 1.0 / float(np.sqrt(8192.0)), 1.0,
                            ALU.mult, ALU.mult)
            nc.all_engine_barrier()
            s.dma_start(d3[:, :, ts(ct, CT)],
                        tob[:, :].rearrange("p (c k) -> p c k", c=NH)).then_inc(s_o, 16)
            sy.wait_ge(s_o, wval(p, sy, [(ct, 16)], 16))
            nc.all_engine_barrier()


def build_w2prep():
    p = Prog()
    w2t = p.inp("w2t", [HID, D], F32)
    h128 = p.inp("h128", [P, P], BF16)
    dst = p.out("w2pt", [HID, D], BF16)
    emit_w2prep(p, w2t, h128, dst)
    return p


def emit_boundary(p, mode, x_in, y_in, ident_h, x_out, xn_out):
    nc = p.nc
    has_y = mode in ("mid", "final")
    has_xn = mode in ("first", "mid")

    yb = p.sb("yb", [P, D], BF16)
    ytm = p.sb("ytm", [P, D], F32)
    sq = p.sb("sq", [P, D], F32)
    xt = p.sb("xt", [P, D], F32)
    xp = p.sb("xp", [P, D], F32)
    xnb = p.sb("xnb", [P, D], BF16)
    xnf = p.sb("xnf", [P, D], BF16)
    idn = p.sb("idn", [P, P], BF16)
    st1 = p.sb("st1", [P, 1], F32); st2 = p.sb("st2", [P, 1], F32)
    mm_ = p.sb("mm_", [P, 1], F32); vv_ = p.sb("vv_", [P, 1], F32)
    t0_ = p.sb("t0_", [P, 1], F32); rr_ = p.sb("rr_", [P, 1], F32)
    psA = p.ps("psA", [P, 2048], F32)

    s_i = p.sem("s_i")
    s_o = p.sem("s_o")
    nd_in = (1 if has_y else 0) + 1
    n_out = (1 if mode != "first" else 0) + (1 if has_xn else 0)

    if True:
        g = nc.gpsimd; te = nc.tensor; s = nc.scalar; v = nc.vector; sy = nc.sync
        sy.dma_start(idn[:, :], ident_h[:, :]).then_inc(s_i, 16)
        sy.wait_ge(s_i, 16)
        nc.all_engine_barrier()
        with myfori(p, ALLE, 0, NT) as tc:
            if has_y:
                sy.dma_start(yb[:, :].rearrange("p (c k) -> p c k", c=ND),
                             y_in.rearrange("(c p) t -> p c t", p=P)[:, :, ts(tc, P)]
                             ).then_inc(s_i, 16)
            s.dma_start(xt[:, :], x_in[ts(tc, P), :]).then_inc(s_i, 16)
            sy.wait_ge(s_i, wval(p, sy, [(tc, nd_in * 16)], (nd_in + 1) * 16))
            nc.all_engine_barrier()
            if has_y:
                for dch in range(ND):
                    te.matmul(psA[:, dch * P:(dch + 1) * P],
                              yb[:, dch * P:(dch + 1) * P], idn[:, :],
                              start=True, stop=True)
                nc.all_engine_barrier()
                s.activation(ytm[:, :], psA[:, :], AF.Copy, accum_out=st1[:, :])
                s.activation(sq[:, :], psA[:, :], AF.Square, accum_out=st2[:, :])
                nc.all_engine_barrier()
                v.tensor_scalar(mm_[:, :], st1[:, :], 1.0 / D, 1.0, ALU.mult, ALU.mult)
                v.tensor_scalar(vv_[:, :], st2[:, :], 1.0 / D, 1.0, ALU.mult, ALU.mult)
                v.tensor_mul(t0_[:, :], mm_[:, :], mm_[:, :])
                v.tensor_sub(vv_[:, :], vv_[:, :], t0_[:, :])
                v.tensor_scalar(vv_[:, :], vv_[:, :], 1.0, EPS, ALU.mult, ALU.add)
                nc.all_engine_barrier()
                s.activation(t0_[:, :], vv_[:, :], AF.Sqrt)
                nc.all_engine_barrier()
                v.reciprocal(rr_[:, :], t0_[:, :])
                v.tensor_scalar(ytm[:, :], ytm[:, :], mm_[:, :], rr_[:, :],
                                ALU.subtract, ALU.mult)
                v.tensor_add(xp[:, :], xt[:, :], ytm[:, :])
                nc.all_engine_barrier()
            else:
                v.tensor_copy(xp[:, :], xt[:, :])
                nc.all_engine_barrier()
            if mode != "first":
                sy.dma_start(x_out[ts(tc, P), :], xp[:, :]).then_inc(s_o, 16)
            if has_xn:
                s.activation(sq[:, :], xp[:, :], AF.Square, accum_out=st2[:, :])
                s.activation(sq[:, :], xp[:, :], AF.Copy, accum_out=st1[:, :])
                nc.all_engine_barrier()
                v.tensor_scalar(mm_[:, :], st1[:, :], 1.0 / D, 1.0, ALU.mult, ALU.mult)
                v.tensor_scalar(vv_[:, :], st2[:, :], 1.0 / D, 1.0, ALU.mult, ALU.mult)
                v.tensor_mul(t0_[:, :], mm_[:, :], mm_[:, :])
                v.tensor_sub(vv_[:, :], vv_[:, :], t0_[:, :])
                v.tensor_scalar(vv_[:, :], vv_[:, :], 1.0, EPS, ALU.mult, ALU.add)
                nc.all_engine_barrier()
                s.activation(t0_[:, :], vv_[:, :], AF.Sqrt)
                nc.all_engine_barrier()
                v.reciprocal(rr_[:, :], t0_[:, :])
                v.tensor_scalar(xnb[:, :], xp[:, :], mm_[:, :], rr_[:, :],
                                ALU.subtract, ALU.mult)
                nc.all_engine_barrier()
                for dch in range(ND):
                    te.matmul(psA[:, dch * P:(dch + 1) * P],
                              xnb[:, dch * P:(dch + 1) * P], idn[:, :],
                              start=True, stop=True)
                nc.all_engine_barrier()
                s.copy(xnf[:, :], psA[:, :])
                nc.all_engine_barrier()
                s.dma_start(xn_out.rearrange("(c p) t -> p c t", p=P)[:, :, ts(tc, P)],
                            xnf[:, :].rearrange("p (c k) -> p c k", c=ND)).then_inc(s_o, 16)
            sy.wait_ge(s_o, wval(p, sy, [(tc, n_out * 16)], n_out * 16))
            nc.all_engine_barrier()


def build_boundary(mode):
    p = Prog()
    has_y = mode in ("mid", "final")
    has_xn = mode in ("first", "mid")
    x_in = p.inp("x", [T, D], F32)
    y_in = p.inp("y", [D, T], BF16) if has_y else None
    ident_h = p.inp("ident", [P, P], BF16)
    x_out = p.out("xo", [T, D], F32) if mode != "first" else None
    xn_out = p.out("xn", [D, T], BF16) if has_xn else None
    emit_boundary(p, mode, x_in, y_in, ident_h, x_out, xn_out)
    return p


def emit_attn(p, nq, nkv, alibi, qf, kf, vtm, ident_h, m0_h, ao, b0_h, sl_h):
    nc = p.nc
    grp = nq // nkv

    qb = p.sb("qb", [P, T], BF16)
    kb = p.sb("kb", [P, T], BF16)
    vb = p.sb("vb", [P, NT * P], BF16)
    idn = p.sb("idn", [P, P], BF16)
    m0s = p.sb("m0s", [P, P], F32)
    biasH = p.sb("biasH", [P, T], F32)
    b0s = p.sb("b0s", [P, T], F32)
    sls = p.sb("sls", [P, 16], F32)
    s2 = p.sb("s2", [P, T], F32)
    probs = p.sb("probs", [P, T], F32)
    probs_b = p.sb("probs_b", [P, T], BF16)
    probsT = p.sb("probsT", [P, T], BF16)
    aosb = p.sb("aosb", [P, T], BF16)
    negmax = p.sb("negmax", [P, 1], F32)
    sums = p.sb("sums", [P, 1], F32)
    recip = p.sb("recip", [P, 1], F32)
    diag = p.sb("diag", [P, P], BF16)
    psA = p.ps("psA", [P, 2048], F32)
    psO = p.ps("psO", [P, P], F32)

    s_i = p.sem("s_i")
    s_o = p.sem("s_o")
    nc0 = 2 + (2 if alibi else 0)

    if True:
        g = nc.gpsimd; te = nc.tensor; s = nc.scalar; v = nc.vector; sy = nc.sync
        sy.dma_start(idn[:, :], ident_h[:, :]).then_inc(s_i, 16)
        s.dma_start(m0s[:, :], m0_h[:, :]).then_inc(s_i, 16)
        if alibi:
            sy.dma_start(b0s[:, :], b0_h[:, :]).then_inc(s_i, 16)
            s.dma_start(sls[:, :], sl_h[:, :]).then_inc(s_i, 16)
        sy.wait_ge(s_i, 16 * nc0)
        nc.all_engine_barrier()
        if not alibi:
            v.memset(biasH[:, :], 0.0)
            nc.all_engine_barrier()

        def head_body(h, kv, hterms, hconst):
            sy.dma_start(qb[:, :], qf[ts(h, P), :]).then_inc(s_i, 16)
            s.dma_start(kb[:, :], kf[ts(kv, P), :]).then_inc(s_i, 16)
            s.dma_start(vb[:, :].rearrange("p (c k) -> p c k", c=NT),
                        vtm.rearrange("a (c p) k -> a p c k", p=P)[kv, :, :, :]
                        ).then_inc(s_i, 16)
            sy.wait_ge(s_i, wval(p, sy, [(t_, 48 * c_) for (t_, c_) in hterms],
                                 16 * (nc0 + 3 * hconst + 3)))
            nc.all_engine_barrier()
            if alibi:
                v.tensor_scalar(biasH[:, :], b0s[:, :], sls[:, ts(h, 1)], 1.0,
                                ALU.mult, ALU.mult)
                nc.all_engine_barrier()
            for qc in range(NT):
                lk = (qc + 1) * P
                nn = (lk + 511) // 512
                for n in range(nn):
                    w_ = min(512, lk - n * 512)
                    te.matmul(psA[:, n * 512:n * 512 + w_],
                              qb[:, qc * P:(qc + 1) * P], kb[:, n * 512:n * 512 + w_],
                              start=True, stop=True)
                nc.all_engine_barrier()
                v.tensor_tensor(s2[:, 0:lk], psA[:, 0:lk], biasH[:, 0:lk], ALU.add)
                v.tensor_tensor(s2[:, lk - P:lk], s2[:, lk - P:lk], m0s[:, :], ALU.add)
                v.reduce_max(negmax[:, :], s2[:, 0:lk], axis=AX.X, negate=True)
                nc.all_engine_barrier()
                s.activation(probs[:, 0:lk], s2[:, 0:lk], AF.Exp,
                             bias=negmax[:, :], accum_out=sums[:, :])
                s.copy(probs_b[:, 0:lk], probs[:, 0:lk])
                nc.all_engine_barrier()
                v.reciprocal(recip[:, :], sums[:, :])
                nc.all_engine_barrier()
                s.activation(diag[:, :], idn[:, :], AF.Copy, scale=recip[:, :])
                nc.all_engine_barrier()
                for kc in range(qc + 1):
                    te.matmul(psA[:, kc * P:(kc + 1) * P],
                              probs_b[:, kc * P:(kc + 1) * P], diag[:, :],
                              start=True, stop=True)
                nc.all_engine_barrier()
                s.copy(probsT[:, 0:lk], psA[:, 0:lk])
                nc.all_engine_barrier()
                for kc in range(qc + 1):
                    te.matmul(psO[:, :], vb[:, kc * P:(kc + 1) * P],
                              probsT[:, kc * P:(kc + 1) * P],
                              start=(kc == 0), stop=(kc == qc))
                nc.all_engine_barrier()
                s.copy(aosb[:, qc * P:(qc + 1) * P], psO[:, :])
                nc.all_engine_barrier()
            s.dma_start(ao[ts(h, P), :], aosb[:, :]).then_inc(s_o, 16)
            sy.wait_ge(s_o, wval(p, sy, [(t_, 16 * c_) for (t_, c_) in hterms],
                                 16 * hconst + 16))
            nc.all_engine_barrier()

        if grp == 1:
            with myfori(p, ALLE, 0, nq) as h:
                head_body(h, h, [(h, 1)], 0)
        else:
            with myfori(p, ALLE, 0, nkv, 0) as kv:
                with myfori(p, ALLE, 0, grp, 1) as gq:
                    h = kv * grp + gq
                    head_body(h, kv, [(kv, grp), (gq, 1)], 0)


def build_attn(nq, nkv, alibi):
    p = Prog()
    qf = p.inp("qf", [nq * P, T], BF16)
    kf = p.inp("kf", [nkv * P, T], BF16)
    vtm = p.inp("vtm", [nkv, T, P], BF16)
    ident_h = p.inp("ident", [P, P], BF16)
    m0_h = p.inp("m0", [P, P], F32)
    ao = p.out("ao", [nq * P, T], BF16)
    b0_h = p.inp("b0", [P, T], F32) if alibi else None
    sl_h = p.inp("slopes", [P, 16], F32) if alibi else None
    emit_attn(p, nq, nkv, alibi, qf, kf, vtm, ident_h, m0_h, ao, b0_h, sl_h)
    return p


def build_fused():
    p = Prog()
    nc = p.nc
    NH = HID // P
    x = p.inp("x", [T, D], F32)
    wq0 = p.inp("wq0", [D, D], F32)
    wk0 = p.inp("wk0", [D, D], F32)
    wv0 = p.inp("wv0", [D, D], F32)
    bq0 = p.inp("bq0", [P, 16], F32)
    bk0 = p.inp("bk0", [P, 16], F32)
    vb0 = p.inp("vb0", [P, D], F32)
    ow0 = p.inp("ow0", [D, D], F32)
    ob0_h = p.inp("ob0", [P, 16], F32)
    sl = p.inp("sl", [P, 16], F32)
    b0c = p.inp("b0c", [P, T], F32)
    m0 = p.inp("m0", [P, P], F32)
    ident = p.inp("ident", [P, P], BF16)
    h128 = p.inp("h128", [P, P], BF16)
    wg0 = p.inp("wg0", [D, HID], I8)
    wu0 = p.inp("wu0", [D, HID], I8)
    wd0 = p.inp("wd0", [HID, D], I8)
    sg = p.inp("sg", [P, NH], F32)
    su = p.inp("su", [P, NH], F32)
    sd = p.inp("sd", [P, ND], F32)
    wq1 = p.inp("wq1", [D, D], F32)
    wk1 = p.inp("wk1", [D, 256], F32)
    wv1 = p.inp("wv1", [D, 256], F32)
    wo1 = p.inp("wo1", [D, D], F32)
    bq1 = p.inp("bq1", [P, 16], F32)
    bk1 = p.inp("bk1", [P, 2], F32)
    vb1 = p.inp("vb1", [P, 256], F32)
    cc = p.inp("cc", [P, T], F32)
    ss = p.inp("ss", [P, T], F32)
    ccq = p.inp("ccq", [P, T], F32)
    ssq = p.inp("ssq", [P, T], F32)
    w1t = p.inp("w1t", [D, HID], F32)
    w3t = p.inp("w3t", [D, HID], F32)
    w2t = p.inp("w2t", [HID, D], F32)
    xn = p.intern("xn_s", [D, T], BF16)
    qf = p.intern("qf_s", [D, T], BF16)
    kfA = p.intern("kfA_s", [D, T], BF16)
    kfB = p.intern("kfB_s", [256, T], BF16)
    vtA = p.intern("vtA_s", [16, T, P], BF16)
    vtB = p.intern("vtB_s", [2, T, P], BF16)
    ao = p.intern("ao_s", [D, T], BF16)
    yb = p.intern("yb_s", [D, T], BF16)
    fuse = p.intern("fuse_s", [HID, T], BF16)
    w2pt = p.intern("w2pt_s", [HID, D], BF16)
    x2 = p.intern("x2_s", [T, D], F32)
    x3 = p.intern("x3_s", [T, D], F32)
    x4 = p.intern("x4_s", [T, D], F32)
    xo = p.out("xo", [T, D], F32)

    phases = [
        ("w2p", lambda: emit_w2prep(p, w2t, h128, w2pt)),
        ("bf", lambda: emit_boundary(p, "first", x, None, ident, None, xn)),
        ("q0", lambda: emit_mm(p, xn, wq0, qf, bq0, None, None, None,
                               D, 16, "bias_scale", scale_imm=SC)),
        ("k0", lambda: emit_mm(p, xn, wk0, kfA, bk0, None, None, None,
                               D, 16, "bias_scale")),
        ("v0", lambda: emit_vtm(p, xn, wv0, vb0, vtA, 2048, 16)),
        ("a0", lambda: emit_attn(p, 16, 16, True, qf, kfA, vtA, ident, m0,
                                 ao, b0c, sl)),
        ("o0", lambda: emit_mm(p, ao, ow0, yb, ob0_h, None, None, None,
                               D, 16, "bias_scale")),
        ("b1", lambda: emit_boundary(p, "mid", x, yb, ident, x2, xn)),
        ("g0", lambda: emit_gateup(p, xn, wg0, wu0, fuse, sg, su)),
        ("d0", lambda: emit_down(p, fuse, wd0, yb, sd)),
        ("b2", lambda: emit_boundary(p, "mid", x2, yb, ident, x3, xn)),
        ("q1", lambda: emit_mm(p, xn, wq1, qf, bq1, None, ccq, ssq,
                               D, 16, "rope")),
        ("k1", lambda: emit_mm(p, xn, wk1, kfB, bk1, None, cc, ss,
                               D, 2, "rope")),
        ("v1", lambda: emit_vtm(p, xn, wv1, vb1, vtB, 256, 2)),
        ("a1", lambda: emit_attn(p, 16, 2, False, qf, kfB, vtB, ident, m0,
                                 ao, None, None)),
        ("o1", lambda: emit_mm(p, ao, wo1, yb, None, None, None, None,
                               D, 16, "plain")),
        ("b3", lambda: emit_boundary(p, "mid", x3, yb, ident, x4, xn)),
        ("g1", lambda: emit_gateup(p, xn, w1t, w3t, fuse, None, None)),
        ("d1", lambda: emit_down(p, fuse, w2pt, yb, None)),
        ("b4", lambda: emit_boundary(p, "final", x4, yb, ident, xo, None)),
    ]
    with p.es:
        for tag, fn in phases:
            p.begin(tag + "_")
            fn()
            nc.all_engine_barrier()
            p.end()
    return p


_PROGS = {}


def _get(name, fn):
    if name not in _PROGS:
        _PROGS[name] = fn()
    return _PROGS[name]


def _b2(a):
    n = a.shape[0] // P
    return np.ascontiguousarray(a.reshape(n, P).T).astype(np.float32)


def kernel(**inp):
    C = _consts()
    x0 = np.ascontiguousarray(np.asarray(inp["L0_attn_hidden_states"], np.float32))

    qkv_wT = np.ascontiguousarray(np.asarray(inp["L0_attn_qkv_w"], np.float32).T)
    qkvb2 = _b2(np.asarray(inp["L0_attn_qkv_b"], np.float32))
    com = {
        "wq0": np.ascontiguousarray(qkv_wT[:, 0:2048]),
        "wk0": np.ascontiguousarray(qkv_wT[:, 2048:4096]),
        "wv0": np.ascontiguousarray(qkv_wT[:, 4096:6144]),
        "bq0": np.ascontiguousarray(qkvb2[:, 0:16]),
        "bk0": np.ascontiguousarray(qkvb2[:, 16:32]),
        "vb0": np.broadcast_to(np.asarray(inp["L0_attn_qkv_b"], np.float32)[4096:],
                               (P, D)).copy(),
        "ow0": np.ascontiguousarray(np.asarray(inp["L0_attn_out_w"], np.float32).T),
        "ob0": _b2(np.asarray(inp["L0_attn_out_b"], np.float32)),
        "sl": np.broadcast_to(np.asarray(inp["L0_attn_alibi_slopes"], np.float32),
                              (P, 16)).copy(),
        "b0c": C["b0"], "m0": C["m0"], "ident": np.asarray(C["ident"]),
        "h128": np.asarray(C["h128"]),
        "wg0": np.ascontiguousarray(np.asarray(inp["L0_ffn_w_gate"], np.int8)),
        "wu0": np.ascontiguousarray(np.asarray(inp["L0_ffn_w_up"], np.int8)),
        "wd0": np.ascontiguousarray(np.asarray(inp["L0_ffn_w_down"], np.int8)),
        "sg": _b2(np.asarray(inp["L0_ffn_s_gate"], np.float32)),
        "su": _b2(np.asarray(inp["L0_ffn_s_up"], np.float32)),
        "sd": _b2(np.asarray(inp["L0_ffn_s_down"], np.float32)),
        "wq1": np.ascontiguousarray(np.asarray(inp["L1_attn_wq"], np.float32).T),
        "wk1": np.ascontiguousarray(np.asarray(inp["L1_attn_wk"], np.float32).T),
        "wv1": np.ascontiguousarray(np.asarray(inp["L1_attn_wv"], np.float32).T),
        "wo1": np.ascontiguousarray(np.asarray(inp["L1_attn_wo"], np.float32).T),
        "bq1": _b2(np.asarray(inp["L1_attn_bq"], np.float32)),
        "bk1": _b2(np.asarray(inp["L1_attn_bk"], np.float32)),
        "vb1": np.broadcast_to(np.asarray(inp["L1_attn_bv"], np.float32),
                               (P, 256)).copy(),
        "cc": C["cc"], "ss": C["ss"], "ccq": C["ccq"], "ssq": C["ssq"],
        "w1t": np.ascontiguousarray(np.asarray(inp["L1_ffn_W1"], np.float32).T),
        "w3t": np.ascontiguousarray(np.asarray(inp["L1_ffn_W3"], np.float32).T),
        "w2t": np.ascontiguousarray(np.asarray(inp["L1_ffn_W2"], np.float32).T),
    }
    pr = _get("fused", build_fused)
    in_maps = [dict(com, x=x0[b]) for b in range(NCORES)]
    res = run_bass_kernel_spmd(pr.nc, in_maps, core_ids=list(range(NCORES)))
    return np.stack([res.results[b]["xo"] for b in range(NCORES)],
                    axis=0).astype(np.float32)


# version 09
# speedup vs baseline: 1.3076x; 1.1190x over previous
import numpy as np
import ml_dtypes
from contextlib import ExitStack

import concourse.bass as bass
import concourse.mybir as mybir
from concourse.bass_utils import run_bass_kernel_spmd

F32 = mybir.dt.float32
BF16 = mybir.dt.bfloat16
I8 = mybir.dt.int8
AF = mybir.ActivationFunctionType
ALU = mybir.AluOpType
AX = mybir.AxisListType
ts = bass.ts
bfdt = ml_dtypes.bfloat16

P = 128
T = 2048
D = 2048
HID = 8192
NT = T // P
ND = D // P
EPS = 1e-5
SC = 1.0 / np.sqrt(128.0)
NCORES = 8
ALLE = mybir.ALL_ENGINES


def _had(n):
    h = np.array([[1.0]])
    while h.shape[0] < n:
        h = np.block([[h, h], [h, -h]])
    return h


def _consts():
    c = {}
    c["ident"] = np.eye(P, dtype=bfdt)
    c["m0"] = np.where(np.arange(P)[None, :] <= np.arange(P)[:, None], 0.0,
                       -1.0e9).astype(np.float32)
    c["b0"] = np.broadcast_to(np.arange(T, dtype=np.float32), (P, T)).copy()
    c["h128"] = _had(128).astype(bfdt)
    half = 64
    inv = 10000.0 ** (-np.arange(half, dtype=np.float64) / half)
    ang = inv[:, None] * np.arange(T, dtype=np.float64)[None, :]
    cc = np.concatenate([np.cos(ang), np.cos(ang)], 0)
    ss = np.concatenate([-np.sin(ang), np.sin(ang)], 0)
    c["cc"] = cc.astype(np.float32)
    c["ss"] = ss.astype(np.float32)
    c["ccq"] = (cc * SC).astype(np.float32)
    c["ssq"] = (ss * SC).astype(np.float32)
    return c


class Prog:
    def __init__(self):
        self.nc = bass.Bass("TRN2", target_bir_lowering=False,
                            detect_race_conditions=False)
        self.es = ExitStack()
        self.cur = self.es
        self.tag = ""
        self.rbs = {}

    def begin(self, tag):
        self.tag = tag
        self.cur = ExitStack()

    def end(self):
        self.cur.close()
        self.cur = self.es
        self.tag = ""

    def inp(self, name, shape, dt=F32):
        return self.nc.dram_tensor(name, list(shape), dt, kind="ExternalInput")

    def out(self, name, shape, dt=F32):
        return self.nc.dram_tensor(name, list(shape), dt, kind="ExternalOutput")

    def intern(self, name, shape, dt=F32):
        return self.nc.dram_tensor(name, list(shape), dt, kind="Internal")

    def sb(self, name, shape, dt):
        return self.cur.enter_context(self.nc.sbuf_tensor(self.tag + name, list(shape), dt))

    def ps(self, name, shape, dt=F32):
        return self.cur.enter_context(self.nc.psum_tensor(self.tag + name, list(shape), dt))

    def sem(self, name):
        return self.nc.alloc_semaphore(self.tag + name)



from concourse.expressions import RuntimeValue
from concourse.bass_primitives import RegisterHandle
from contextlib import contextmanager


class _RB:

    def __init__(self, eng):
        self.eng = eng
        self.loops = [eng.alloc_register(f"mfl{i}") for i in range(2)]
        self.tmps = [eng.alloc_register(f"mft{i}") for i in range(6)]
        self.ti = 0

    def tmp(self):
        r = self.tmps[self.ti % 6]
        self.ti += 1
        return r


def _rb(p, eng):
    key = eng.engine
    if key not in p.rbs:
        p.rbs[key] = _RB(eng)
    return p.rbs[key]


def wval(p, eng, terms, const, vmin=None, vmax=None):
    rb = _rb(p, eng)
    r = rb.tmp()
    lo = hi = const
    if not terms:
        return const
    (rv0, c0) = terms[0]
    eng.reg_alu(r, rv0, c0, ALU.mult)
    lo += (rv0.min_val or 0) * c0
    hi += (rv0.max_val or 0) * c0
    for rv, c in terms[1:]:
        rb2 = _rb(p, eng)
        t2 = rb2.tmp()
        eng.reg_alu(t2, rv, c, ALU.mult)
        eng.reg_alu(r, r, t2, ALU.add)
        lo += (rv.min_val or 0) * c
        hi += (rv.max_val or 0) * c
    if const:
        eng.reg_alu(r, r, const, ALU.add)
    return RuntimeValue(r, min_val=vmin if vmin is not None else lo,
                        max_val=vmax if vmax is not None else hi)


@contextmanager
def myfori(p, engines, start, end, lvl=0):
    nc = p.nc
    engs = [nc.engines[e] if not hasattr(e, "engine") else e for e in engines]
    regs = [_rb(p, e).loops[lvl] for e in engs]
    loop_id = nc.next_id()
    loop_start = f"mf_{loop_id}_loop"
    loop_end = f"mf_{loop_id}_end"
    etypes = [e.engine for e in engs]
    for e, r in zip(engs, regs):
        e.reg_mov(r, start)
    nc.br(loop_start, engines=etypes)
    from concourse.bass_primitives import RegisterHandles as RHs
    handles = RHs(regs)
    with nc.body(loop_start, valid_engines=etypes):
        yield RuntimeValue(handles, min_val=start, max_val=end - 1)
        for e, r in zip(engs, regs):
            e.reg_alu(r, r, 1, ALU.add)
        nc.br_cmp(handles, end, on_true=loop_start, on_false=loop_end,
                  comp_op="IS_LT", engines=etypes)
    nc.switch_bb(loop_end)


def emit_mm(p, xn, w, dst, bias_h, scol_h, cc_h, ss_h,
            din, dout_chunks, epi, tiles=1, scale_imm=None, act_fn="copy"):
    rope = (epi == "rope")
    has_bias = bias_h is not None
    act_scale = scol_h is not None
    nc = p.nc
    nki = din // P
    TT_ = T // tiles
    w3 = w.rearrange("(i p) o -> p i o", p=P)

    xnb = p.sb("xnb", [P, nki * TT_], BF16)
    wb = [p.sb(f"wb{i}", [P, nki * P], BF16) for i in range(2)]
    tmpf = [p.sb(f"tmpf{i}", [P, TT_], F32) for i in range(2)]
    ob = [p.sb(f"ob{i}", [P, TT_], BF16) for i in range(2)]
    psb = [p.ps(f"psb{i}", [P, min(TT_, 2048)], F32) for i in range(2)]
    bias_sb = p.sb("bias_sb", [P, dout_chunks], F32)
    scol_sb = p.sb("scol_sb", [P, dout_chunks], F32)
    if rope:
        cc_sb = p.sb("cc_sb", [P, T], F32)
        ss_sb = p.sb("ss_sb", [P, T], F32)
        rtmp = p.sb("rtmp", [P, T], F32)
        rtm2 = p.sb("rtm2", [P, T], F32)

    s_x = p.sem("s_x")
    s_w = [p.sem(f"s_w{i}") for i in range(2)]
    s_mm = p.sem("s_mm")
    s_se = p.sem("s_se")
    s_g = p.sem("s_g")
    s_st = p.sem("s_st")

    jobs = [(tt, o) for tt in range(tiles) for o in range(dout_chunks)]
    njob = len(jobs)
    nconst = (1 if has_bias else 0) + (1 if act_scale else 0) + (2 if rope else 0)
    g_is_epi = epi in ("bias_scale", "rope")

    with nc.Block() as block:
        @block.gpsimd
        def _(g_):
            if has_bias:
                g_.dma_start(bias_sb[:, :], bias_h[:, :]).then_inc(s_x, 16)
            if act_scale:
                g_.dma_start(scol_sb[:, :], scol_h[:, :]).then_inc(s_x, 16)
            if rope:
                g_.dma_start(cc_sb[:, :], cc_h[:, :]).then_inc(s_x, 16)
                g_.dma_start(ss_sb[:, :], ss_h[:, :]).then_inc(s_x, 16)
            nx = 0
            for k, (tt, o) in enumerate(jobs):
                par = k % 2
                if o == 0:
                    if k > 0:
                        g_.wait_ge(s_se, k)
                    g_.dma_start(
                        xnb[:, :].rearrange("p (i t) -> p i t", i=nki),
                        xn.rearrange("(i p) t -> p i t", p=P)[:, :, ts(tt, TT_)]
                    ).then_inc(s_x, 16)
                    nx += 1
                if k >= 2:
                    g_.wait_ge(s_mm, k - 1)
                g_.dma_start(wb[par][:, :].rearrange("p (i c) -> p i c", i=nki),
                             w3[:, :, ts(o, P)]).then_inc(s_w[par], 16)
                if g_is_epi:
                    g_.wait_ge(s_se, k + 1)
                    if k >= 2:
                        g_.wait_ge(s_st, 16 * (k - 1))
                    src = tmpf[par]
                    if epi == "bias_scale":
                        if has_bias:
                            ge = g_.tensor_scalar(ob[par][:, :], src[:, :],
                                                  bias_sb[:, o:o + 1],
                                                  scale_imm or 1.0, ALU.add, ALU.mult)
                        else:
                            ge = g_.tensor_scalar(ob[par][:, :], src[:, :],
                                                  scale_imm or 1.0, 1.0,
                                                  ALU.mult, ALU.mult)
                    else:
                        g_.tensor_scalar(rtmp[:, :], src[:, :],
                                         bias_sb[:, o:o + 1], 1.0, ALU.add, ALU.mult)
                        g_.tensor_copy(rtm2[0:64, :], rtmp[64:128, :])
                        g_.tensor_copy(rtm2[64:128, :], rtmp[0:64, :])
                        g_.tensor_mul(rtmp[:, :], rtmp[:, :], cc_sb[:, :])
                        g_.tensor_mul(rtm2[:, :], rtm2[:, :], ss_sb[:, :])
                        ge = g_.tensor_add(ob[par][:, :], rtmp[:, :], rtm2[:, :])
                    ge.then_inc(s_g, 1)

        @block.tensor
        def _(t_):
            nx = 0
            for k, (tt, o) in enumerate(jobs):
                par = k % 2
                if o == 0:
                    nx += 1
                t_.wait_ge(s_x, 16 * (nconst + nx))
                t_.wait_ge(s_w[par], 16 * (k // 2 + 1))
                if k >= 2:
                    t_.wait_ge(s_se, k - 1)
                nn = (TT_ + 511) // 512
                for n in range(nn):
                    w_ = min(512, TT_ - n * 512)
                    for i in range(nki):
                        mm = t_.matmul(psb[par][:, n * 512:n * 512 + w_],
                                       wb[par][:, i * P:(i + 1) * P],
                                       xnb[:, i * TT_ + n * 512: i * TT_ + n * 512 + w_],
                                       start=(i == 0), stop=(i == nki - 1))
                mm.then_inc(s_mm, 1)

        @block.scalar
        def _(s_):
            for k, (tt, o) in enumerate(jobs):
                par = k % 2
                s_.wait_ge(s_mm, k + 1)
                if g_is_epi:
                    if k >= 2:
                        s_.wait_ge(s_g, k - 1)
                    s_.copy(tmpf[par][:, :], psb[par][:, :]).then_inc(s_se, 1)
                else:
                    if k >= 2:
                        s_.wait_ge(s_st, 16 * (k - 1))
                    if act_scale:
                        fn = AF.Silu if act_fn == "silu" else AF.Copy
                        s_.activation(ob[par][:, :], psb[par][:, :], fn,
                                      scale=scol_sb[:, o:o + 1]).then_inc(s_se, 1)
                    else:
                        s_.copy(ob[par][:, :], psb[par][:, :]).then_inc(s_se, 1)

        @block.sync
        def _(sy):
            for k, (tt, o) in enumerate(jobs):
                par = k % 2
                sy.wait_ge(s_g if g_is_epi else s_se, k + 1)
                if tiles == 1:
                    sy.dma_start(dst[ts(o, P), :], ob[par][:, :]).then_inc(s_st, 16)
                else:
                    sy.dma_start(dst[ts(o, P), ts(tt, TT_)],
                                 ob[par][:, :]).then_inc(s_st, 16)
            sy.wait_ge(s_st, 16 * njob)


def build_mm_phase(din, dout_chunks, wdt, epi, tiles=1, has_bias=False,
                   scale_imm=None, act_scale=False, act_fn="copy"):
    rope = (epi == "rope")
    p = Prog()
    xn = p.inp("xn", [din, T], BF16)
    w = p.inp("w", [din, dout_chunks * P], wdt)
    dst = p.out("o", [dout_chunks * P, T], BF16)
    bias_h = p.inp("bias", [P, dout_chunks], F32) if has_bias else None
    scol_h = p.inp("scol", [P, dout_chunks], F32) if act_scale else None
    cc_h = p.inp("cc", [P, T], F32) if rope else None
    ss_h = p.inp("ss", [P, T], F32) if rope else None
    emit_mm(p, xn, w, dst, bias_h, scol_h, cc_h, ss_h, din, dout_chunks, epi,
            tiles, scale_imm, act_fn)
    return p


def emit_vtm(p, xn, w, vb, dst, dout, nkvh):
    nc = p.nc
    nki = ND
    w3 = w.rearrange("(i p) o -> p i o", p=P)

    xnb = p.sb("xnb", [P, nki * T], BF16)
    wball = p.sb("wball", [P, nki * dout], BF16)
    vb_sb = p.sb("vb_sb", [P, dout], F32)
    tmpf = [p.sb(f"tmpf{i}", [P, dout], F32) for i in range(2)]
    ob = [p.sb(f"ob{i}", [P, dout], BF16) for i in range(2)]
    nn = (dout + 511) // 512
    psb = [p.ps(f"psb{i}", [P, min(dout, 2048)], F32) for i in range(2)]

    s_x = p.sem("s_x"); s_mm = p.sem("s_mm"); s_se = p.sem("s_se")
    s_g = p.sem("s_g"); s_st = p.sem("s_st")

    with nc.Block() as block:
        @block.gpsimd
        def _(g_):
            g_.dma_start(vb_sb[:, :], vb[:, :]).then_inc(s_x, 16)
            g_.dma_start(xnb[:, :].rearrange("p (i t) -> p i t", i=nki),
                         xn.rearrange("(i p) t -> p i t", p=P)).then_inc(s_x, 16)
            g_.dma_start(wball[:, :].rearrange("p (i c) -> p i c", i=nki),
                         w3[:, :, :]).then_inc(s_x, 16)
            for k in range(NT):
                par = k % 2
                g_.wait_ge(s_se, k + 1)
                if k >= 2:
                    g_.wait_ge(s_st, 16 * nkvh * (k - 1))
                g_.tensor_add(ob[par][:, :], tmpf[par][:, :],
                              vb_sb[:, :]).then_inc(s_g, 1)

        @block.tensor
        def _(t_):
            t_.wait_ge(s_x, 48)
            for k in range(NT):
                par = k % 2
                if k >= 2:
                    t_.wait_ge(s_se, k - 1)
                for n in range(nn):
                    w_ = min(512, dout - n * 512)
                    for i in range(nki):
                        mm = t_.matmul(psb[par][:, n * 512:n * 512 + w_],
                                       xnb[:, i * T + k * P: i * T + (k + 1) * P],
                                       wball[:, i * dout + n * 512: i * dout + n * 512 + w_],
                                       start=(i == 0), stop=(i == nki - 1))
                mm.then_inc(s_mm, 1)

        @block.scalar
        def _(s_):
            for k in range(NT):
                par = k % 2
                s_.wait_ge(s_mm, k + 1)
                if k >= 2:
                    s_.wait_ge(s_g, k - 1)
                s_.copy(tmpf[par][:, :], psb[par][:, :]).then_inc(s_se, 1)

        @block.sync
        def _(sy):
            for k in range(NT):
                par = k % 2
                sy.wait_ge(s_g, k + 1)
                for h in range(nkvh):
                    sy.dma_start(dst[h, ts(k, P), :],
                                 ob[par][:, h * P:(h + 1) * P]).then_inc(s_st, 16)
            sy.wait_ge(s_st, 16 * nkvh * NT)


def build_vtm(dout, nkvh):
    p = Prog()
    xn = p.inp("xn", [D, T], BF16)
    w = p.inp("w", [D, dout], F32)
    vb = p.inp("vbias", [P, dout], F32)
    dst = p.out("vtm", [nkvh, T, P], BF16)
    emit_vtm(p, xn, w, vb, dst, dout, nkvh)
    return p


def emit_gateup(p, xn, wg, wu, dst, sg_h, su_h):
    with_scales = sg_h is not None
    nc = p.nc
    nki = ND
    NH = HID // P
    wg3 = wg.rearrange("(i p) o -> p i o", p=P)
    wu3 = wu.rearrange("(i p) o -> p i o", p=P)

    xnb = p.sb("xnb", [P, nki * T], BF16)
    wgb = [p.sb(f"wgb{i}", [P, nki * P], BF16) for i in range(2)]
    wub = [p.sb(f"wub{i}", [P, nki * P], BF16) for i in range(2)]
    gsb = [p.sb(f"gsb{i}", [P, 1024], F32) for i in range(2)]
    usb = [p.sb(f"usb{i}", [P, 1024], F32) for i in range(2)]
    fsb = [p.sb(f"fsb{i}", [P, T], BF16) for i in range(2)]
    sg_sb = p.sb("sg_sb", [P, NH], F32)
    su_sb = p.sb("su_sb", [P, NH], F32)
    psg = [p.ps(f"psg{i}", [P, 1024], F32) for i in range(2)]
    psu = [p.ps(f"psu{i}", [P, 1024], F32) for i in range(2)]

    s_x = p.sem("s_x"); s_w = [p.sem(f"s_w{i}") for i in range(2)]
    s_mm = p.sem("s_mm"); s_se = p.sem("s_se"); s_ve = p.sem("s_ve")
    s_st = p.sem("s_st")

    nconst = 2 if with_scales else 0

    with p.es, nc.Block() as block:
        @block.gpsimd
        def _(g_):
            if with_scales:
                g_.dma_start(sg_sb[:, :], sg_h[:, :]).then_inc(s_x, 16)
                g_.dma_start(su_sb[:, :], su_h[:, :]).then_inc(s_x, 16)
            g_.dma_start(xnb[:, :].rearrange("p (i t) -> p i t", i=nki),
                         xn.rearrange("(i p) t -> p i t", p=P)).then_inc(s_x, 16)

            def body(ch, first):
                for cpar in range(2):
                    c = ch * 2 + cpar
                    if not first:
                        g_.wait_ge(s_mm, wval(p, g_, [(ch, 4)], -2 + 2 * cpar))
                    g_.dma_start(wgb[cpar][:, :].rearrange("p (i c) -> p i c", i=nki),
                                 wg3[:, :, ts(c, P)]).then_inc(s_w[cpar], 16)
                    g_.dma_start(wub[cpar][:, :].rearrange("p (i c) -> p i c", i=nki),
                                 wu3[:, :, ts(c, P)]).then_inc(s_w[cpar], 16)
                for cpar in range(2):
                    for hf in range(2):
                        if first:
                            g_.wait_ge(s_se, 2 * (2 * cpar + hf + 1))
                        else:
                            g_.wait_ge(s_se, wval(p, g_, [(ch, 8)],
                                                  2 * (2 * cpar + hf + 1)))
                            if hf == 0:
                                g_.wait_ge(s_st, wval(p, g_, [(ch, 32)],
                                                      16 * (cpar - 1)))
                        g_.tensor_mul(fsb[cpar][:, hf * 1024:(hf + 1) * 1024],
                                      gsb[hf][:, :], usb[hf][:, :]).then_inc(s_ve, 1)
            body(0, True)
            with myfori(p, [g_], 1, NH // 2) as ch:
                body(ch, False)

        @block.tensor
        def _(t_):
            t_.wait_ge(s_x, 16 * (nconst + 1))

            def job(ch, cpar, hf, first):
                pp = hf
                if isinstance(ch, int):
                    t_.wait_ge(s_w[cpar], 32 * (ch + 1))
                    if not first:
                        t_.wait_ge(s_se, 2 * (4 * ch + 2 * cpar + hf - 1))
                else:
                    t_.wait_ge(s_w[cpar], wval(p, t_, [(ch, 32)], 32))
                    t_.wait_ge(s_se, wval(p, t_, [(ch, 8)],
                                          2 * (2 * cpar + hf - 1)))
                for n in range(2):
                    off = hf * 1024 + n * 512
                    for i in range(nki):
                        t_.matmul(psg[pp][:, n * 512:(n + 1) * 512],
                                  wgb[cpar][:, i * P:(i + 1) * P],
                                  xnb[:, i * T + off: i * T + off + 512],
                                  start=(i == 0), stop=(i == nki - 1))
                for n in range(2):
                    off = hf * 1024 + n * 512
                    for i in range(nki):
                        mm = t_.matmul(psu[pp][:, n * 512:(n + 1) * 512],
                                       wub[cpar][:, i * P:(i + 1) * P],
                                       xnb[:, i * T + off: i * T + off + 512],
                                       start=(i == 0), stop=(i == nki - 1))
                mm.then_inc(s_mm, 1)
            for cpar in range(2):
                for hf in range(2):
                    job(0, cpar, hf, 2 * cpar + hf < 2)
            with myfori(p, [t_], 1, NH // 2) as ch:
                for cpar in range(2):
                    for hf in range(2):
                        job(ch, cpar, hf, False)

        @block.scalar
        def _(s_):
            for c in range(NH):
                for hf in range(2):
                    k = 2 * c + hf
                    pp = hf
                    s_.wait_ge(s_mm, k + 1)
                    if k >= 2:
                        s_.wait_ge(s_ve, k - 1)
                    if with_scales:
                        s_.activation(gsb[pp][:, :], psg[pp][:, :], AF.Silu,
                                      scale=sg_sb[:, c:c + 1])
                        s_.activation(usb[pp][:, :], psu[pp][:, :], AF.Copy,
                                      scale=su_sb[:, c:c + 1]).then_inc(s_se, 2)
                    else:
                        s_.activation(gsb[pp][:, :], psg[pp][:, :], AF.Silu)
                        s_.copy(usb[pp][:, :], psu[pp][:, :]).then_inc(s_se, 2)
                if c >= 1:
                    s_.wait_ge(s_ve, 2 * c)
                    s_.dma_start(dst[(c - 1) * P:c * P, :],
                                 fsb[(c - 1) % 2][:, :]).then_inc(s_st, 16)
            s_.wait_ge(s_ve, 2 * NH)
            s_.dma_start(dst[(NH - 1) * P:NH * P, :],
                         fsb[(NH - 1) % 2][:, :]).then_inc(s_st, 16)
            s_.wait_ge(s_st, 16 * NH)


def build_gateup(wdt, with_scales):
    p = Prog()
    xn = p.inp("xn", [D, T], BF16)
    wg = p.inp("wg", [D, HID], wdt)
    wu = p.inp("wu", [D, HID], wdt)
    dst = p.out("fuse", [HID, T], BF16)
    NH = HID // P
    sg_h = p.inp("sg", [P, NH], F32) if with_scales else None
    su_h = p.inp("su", [P, NH], F32) if with_scales else None
    emit_gateup(p, xn, wg, wu, dst, sg_h, su_h)
    return p


def emit_down(p, fuse, wd, dst, sd_h):
    with_scale = sd_h is not None
    nc = p.nc
    NH = HID // P
    TT_ = 512
    tiles = T // TT_
    wd3 = wd.rearrange("(i p) o -> p i o", p=P)

    fub = p.sb("fub", [P, NH * TT_], BF16)
    wdb = [p.sb(f"wdb{i}", [P, NH * P], BF16) for i in range(2)]
    ob = [p.sb(f"ob{i}", [P, TT_], BF16) for i in range(2)]
    sd_sb = p.sb("sd_sb", [P, ND], F32)
    psb = [p.ps(f"psb{i}", [P, TT_], F32) for i in range(2)]

    s_x = p.sem("s_x"); s_w = [p.sem(f"s_w{i}") for i in range(2)]
    s_mm = p.sem("s_mm"); s_se = p.sem("s_se"); s_st = p.sem("s_st")
    s_so = p.sem("s_so")

    nconst = 1 if with_scale else 0
    NOP = ND // 2

    with p.es, nc.Block() as block:
        @block.gpsimd
        def _(g_):
            if with_scale:
                g_.dma_start(sd_sb[:, :], sd_h[:, :]).then_inc(s_x, 16)

            def fload(tt):
                g_.dma_start(fub[:, :].rearrange("p (i t) -> p i t", i=NH),
                             fuse.rearrange("(i p) t -> p i t", p=P)[:, :, ts(tt, TT_)]
                             ).then_inc(s_x, 16)

            def body(tt, op, first):
                for par in range(2):
                    if not first or par == 1:
                        if isinstance(tt, int) and isinstance(op, int):
                            g_.wait_ge(s_mm, tt * ND + op * 2 + par - 1)
                        else:
                            tm = [(tt, ND)] if not isinstance(tt, int) else []
                            tm += [(op, 2)] if not isinstance(op, int) else []
                            cst = (tt * ND if isinstance(tt, int) else 0) + \
                                  (op * 2 if isinstance(op, int) else 0) + par - 1
                            g_.wait_ge(s_mm, wval(p, g_, tm, cst))
                    g_.dma_start(wdb[par][:, :].rearrange("p (i c) -> p i c", i=NH),
                                 wd3[:, :, ts(op * 2 + par, P)]).then_inc(s_w[par], 16)
            fload(0)
            body(0, 0, True)
            with myfori(p, [g_], 1, NOP, 1) as op:
                body(0, op, False)
            with myfori(p, [g_], 1, tiles, 0) as tt:
                g_.wait_ge(s_se, wval(p, g_, [(tt, ND)], 0))
                fload(tt)
                with myfori(p, [g_], 0, NOP, 1) as op:
                    body(tt, op, False)

        @block.tensor
        def _(t_):
            def body(tt, op, first):
                for par in range(2):
                    if isinstance(tt, int) and isinstance(op, int):
                        t_.wait_ge(s_w[par], 16 * (tt * NOP + op + 1))
                        if not first:
                            t_.wait_ge(s_se, tt * ND + op * 2 + par - 1)
                    else:
                        tm = [(tt, 16 * NOP)] if not isinstance(tt, int) else []
                        tm += [(op, 16)] if not isinstance(op, int) else []
                        cst = (16 * tt * NOP if isinstance(tt, int) else 0) + \
                              (16 * op if isinstance(op, int) else 0) + 16
                        t_.wait_ge(s_w[par], wval(p, t_, tm, cst))
                        tm2 = [(tt, ND)] if not isinstance(tt, int) else []
                        tm2 += [(op, 2)] if not isinstance(op, int) else []
                        cst2 = (tt * ND if isinstance(tt, int) else 0) + \
                               (op * 2 if isinstance(op, int) else 0) + par - 1
                        t_.wait_ge(s_se, wval(p, t_, tm2, cst2))
                    for i in range(NH):
                        mm = t_.matmul(psb[par][:, :],
                                       wdb[par][:, i * P:(i + 1) * P],
                                       fub[:, i * TT_:(i + 1) * TT_],
                                       start=(i == 0), stop=(i == NH - 1))
                    mm.then_inc(s_mm, 1)
            t_.wait_ge(s_x, 16 * (nconst + 1))
            body(0, 0, True)
            with myfori(p, [t_], 1, NOP, 1) as op:
                body(0, op, False)
            with myfori(p, [t_], 1, tiles, 0) as tt:
                t_.wait_ge(s_x, wval(p, t_, [(tt, 16)], 16 * (nconst + 1)))
                with myfori(p, [t_], 0, NOP, 1) as op:
                    body(tt, op, False)

        @block.scalar
        def _(s_):
            for tt in range(tiles):
                for o in range(ND):
                    par = o % 2
                    k = tt * ND + o
                    s_.wait_ge(s_mm, k + 1)
                    if k >= 2:
                        s_.wait_ge(s_so, 16 * (k - 1))
                    if with_scale:
                        s_.activation(ob[par][:, :], psb[par][:, :], AF.Copy,
                                      scale=sd_sb[:, o:o + 1]).then_inc(s_se, 1)
                    else:
                        s_.copy(ob[par][:, :], psb[par][:, :]).then_inc(s_se, 1)
                    s_.wait_ge(s_se, k + 1)
                    s_.dma_start(dst[o * P:(o + 1) * P, tt * 512:(tt + 1) * 512],
                                 ob[par][:, :]).then_inc(s_so, 16)
            s_.wait_ge(s_so, 16 * tiles * ND)


def build_down(wdt, with_scale):
    p = Prog()
    fuse = p.inp("fuse", [HID, T], BF16)
    wd = p.inp("wd", [HID, D], wdt)
    dst = p.out("y", [D, T], BF16)
    sd_h = p.inp("sd", [P, ND], F32) if with_scale else None
    emit_down(p, fuse, wd, dst, sd_h)
    return p


def emit_w2prep(p, w2t, h128, dst):
    nc = p.nc
    CT = 256
    nct = D // CT
    NH = HID // P
    w3 = w2t.rearrange("(c p) o -> p c o", p=P)
    d3 = dst.rearrange("(c p) o -> p c o", p=P)

    win = p.sb("win", [P, NH * CT], BF16)
    ta = p.sb("ta", [P, NH * CT], F32)
    tb = p.sb("tb", [P, NH * CT], F32)
    tob = p.sb("tob", [P, NH * CT], BF16)
    h_sb = p.sb("h_sb", [P, P], BF16)
    psq = p.ps("psq", [P, 2048], F32)

    s_d = p.sem("s_d")
    s_o = p.sem("s_o")

    if True:
        g = nc.gpsimd; te = nc.tensor; s = nc.scalar; sy = nc.sync
        v = nc.vector
        g.dma_start(h_sb[:, :], h128[:, :]).then_inc(s_d, 16)
        g.wait_ge(s_d, 16)
        nc.all_engine_barrier()
        with myfori(p, ALLE, 0, nct) as ct:
            g.dma_start(win[:, :].rearrange("p (c k) -> p c k", c=NH),
                        w3[:, :, ts(ct, CT)]).then_inc(s_d, 16)
            g.wait_ge(s_d, wval(p, g, [(ct, 16)], 32))
            nc.all_engine_barrier()
            for rnd in range(NH // 8):
                for j in range(8):
                    c = rnd * 8 + j
                    te.matmul(psq[:, j * CT:(j + 1) * CT], h_sb[:, :],
                              win[:, c * CT:(c + 1) * CT], start=True, stop=True)
                nc.all_engine_barrier()
                for j in range(8):
                    c = rnd * 8 + j
                    s.copy(ta[:, c * CT:(c + 1) * CT], psq[:, j * CT:(j + 1) * CT])
                nc.all_engine_barrier()
            src, dsts = ta, tb
            for st in range(6):
                hc = 1 << st
                sv_ = src[:, :].rearrange("p (g two j k) -> p g two j k",
                                          two=2, j=hc, k=CT)
                dv_ = dsts[:, :].rearrange("p (g two j k) -> p g two j k",
                                           two=2, j=hc, k=CT)
                v.tensor_add(dv_[:, :, 0, :, :], sv_[:, :, 0, :, :], sv_[:, :, 1, :, :])
                v.tensor_sub(dv_[:, :, 1, :, :], sv_[:, :, 0, :, :], sv_[:, :, 1, :, :])
                nc.all_engine_barrier()
                src, dsts = dsts, src
            v.tensor_scalar(tob[:, :], src[:, :], 1.0 / float(np.sqrt(8192.0)), 1.0,
                            ALU.mult, ALU.mult)
            nc.all_engine_barrier()
            s.dma_start(d3[:, :, ts(ct, CT)],
                        tob[:, :].rearrange("p (c k) -> p c k", c=NH)).then_inc(s_o, 16)
            sy.wait_ge(s_o, wval(p, sy, [(ct, 16)], 16))
            nc.all_engine_barrier()


def build_w2prep():
    p = Prog()
    w2t = p.inp("w2t", [HID, D], F32)
    h128 = p.inp("h128", [P, P], BF16)
    dst = p.out("w2pt", [HID, D], BF16)
    emit_w2prep(p, w2t, h128, dst)
    return p


def emit_boundary(p, mode, x_in, y_in, ident_h, x_out, xn_out):
    nc = p.nc
    has_y = mode in ("mid", "final")
    has_xn = mode in ("first", "mid")

    yb = p.sb("yb", [P, D], BF16)
    ytm = p.sb("ytm", [P, D], F32)
    sq = p.sb("sq", [P, D], F32)
    xt = p.sb("xt", [P, D], F32)
    xp = p.sb("xp", [P, D], F32)
    xnb = p.sb("xnb", [P, D], BF16)
    xnf = p.sb("xnf", [P, D], BF16)
    idn = p.sb("idn", [P, P], BF16)
    st1 = p.sb("st1", [P, 1], F32); st2 = p.sb("st2", [P, 1], F32)
    mm_ = p.sb("mm_", [P, 1], F32); vv_ = p.sb("vv_", [P, 1], F32)
    t0_ = p.sb("t0_", [P, 1], F32); rr_ = p.sb("rr_", [P, 1], F32)
    psA = p.ps("psA", [P, 2048], F32)

    s_i = p.sem("s_i")
    s_o = p.sem("s_o")
    nd_in = (1 if has_y else 0) + 1
    n_out = (1 if mode != "first" else 0) + (1 if has_xn else 0)

    if True:
        g = nc.gpsimd; te = nc.tensor; s = nc.scalar; v = nc.vector; sy = nc.sync
        sy.dma_start(idn[:, :], ident_h[:, :]).then_inc(s_i, 16)
        sy.wait_ge(s_i, 16)
        nc.all_engine_barrier()
        with myfori(p, ALLE, 0, NT) as tc:
            if has_y:
                sy.dma_start(yb[:, :].rearrange("p (c k) -> p c k", c=ND),
                             y_in.rearrange("(c p) t -> p c t", p=P)[:, :, ts(tc, P)]
                             ).then_inc(s_i, 16)
            s.dma_start(xt[:, :], x_in[ts(tc, P), :]).then_inc(s_i, 16)
            sy.wait_ge(s_i, wval(p, sy, [(tc, nd_in * 16)], (nd_in + 1) * 16))
            nc.all_engine_barrier()
            if has_y:
                for dch in range(ND):
                    te.matmul(psA[:, dch * P:(dch + 1) * P],
                              yb[:, dch * P:(dch + 1) * P], idn[:, :],
                              start=True, stop=True)
                nc.all_engine_barrier()
                s.activation(ytm[:, :], psA[:, :], AF.Copy, accum_out=st1[:, :])
                s.activation(sq[:, :], psA[:, :], AF.Square, accum_out=st2[:, :])
                nc.all_engine_barrier()
                v.tensor_scalar(mm_[:, :], st1[:, :], 1.0 / D, 1.0, ALU.mult, ALU.mult)
                v.tensor_scalar(vv_[:, :], st2[:, :], 1.0 / D, 1.0, ALU.mult, ALU.mult)
                v.tensor_mul(t0_[:, :], mm_[:, :], mm_[:, :])
                v.tensor_sub(vv_[:, :], vv_[:, :], t0_[:, :])
                v.tensor_scalar(vv_[:, :], vv_[:, :], 1.0, EPS, ALU.mult, ALU.add)
                nc.all_engine_barrier()
                s.activation(t0_[:, :], vv_[:, :], AF.Sqrt)
                nc.all_engine_barrier()
                v.reciprocal(rr_[:, :], t0_[:, :])
                v.tensor_scalar(ytm[:, :], ytm[:, :], mm_[:, :], rr_[:, :],
                                ALU.subtract, ALU.mult)
                v.tensor_add(xp[:, :], xt[:, :], ytm[:, :])
                nc.all_engine_barrier()
            else:
                v.tensor_copy(xp[:, :], xt[:, :])
                nc.all_engine_barrier()
            if mode != "first":
                sy.dma_start(x_out[ts(tc, P), :], xp[:, :]).then_inc(s_o, 16)
            if has_xn:
                s.activation(sq[:, :], xp[:, :], AF.Square, accum_out=st2[:, :])
                s.activation(sq[:, :], xp[:, :], AF.Copy, accum_out=st1[:, :])
                nc.all_engine_barrier()
                v.tensor_scalar(mm_[:, :], st1[:, :], 1.0 / D, 1.0, ALU.mult, ALU.mult)
                v.tensor_scalar(vv_[:, :], st2[:, :], 1.0 / D, 1.0, ALU.mult, ALU.mult)
                v.tensor_mul(t0_[:, :], mm_[:, :], mm_[:, :])
                v.tensor_sub(vv_[:, :], vv_[:, :], t0_[:, :])
                v.tensor_scalar(vv_[:, :], vv_[:, :], 1.0, EPS, ALU.mult, ALU.add)
                nc.all_engine_barrier()
                s.activation(t0_[:, :], vv_[:, :], AF.Sqrt)
                nc.all_engine_barrier()
                v.reciprocal(rr_[:, :], t0_[:, :])
                v.tensor_scalar(xnb[:, :], xp[:, :], mm_[:, :], rr_[:, :],
                                ALU.subtract, ALU.mult)
                nc.all_engine_barrier()
                for dch in range(ND):
                    te.matmul(psA[:, dch * P:(dch + 1) * P],
                              xnb[:, dch * P:(dch + 1) * P], idn[:, :],
                              start=True, stop=True)
                nc.all_engine_barrier()
                s.copy(xnf[:, :], psA[:, :])
                nc.all_engine_barrier()
                s.dma_start(xn_out.rearrange("(c p) t -> p c t", p=P)[:, :, ts(tc, P)],
                            xnf[:, :].rearrange("p (c k) -> p c k", c=ND)).then_inc(s_o, 16)
            sy.wait_ge(s_o, wval(p, sy, [(tc, n_out * 16)], n_out * 16))
            nc.all_engine_barrier()


def build_boundary(mode):
    p = Prog()
    has_y = mode in ("mid", "final")
    has_xn = mode in ("first", "mid")
    x_in = p.inp("x", [T, D], F32)
    y_in = p.inp("y", [D, T], BF16) if has_y else None
    ident_h = p.inp("ident", [P, P], BF16)
    x_out = p.out("xo", [T, D], F32) if mode != "first" else None
    xn_out = p.out("xn", [D, T], BF16) if has_xn else None
    emit_boundary(p, mode, x_in, y_in, ident_h, x_out, xn_out)
    return p


ATTN_SERIAL = False


def emit_attn(p, nq, nkv, alibi, qf, kf, vtm, ident_h, m0_h, ao, b0_h, sl_h):
    nc = p.nc
    grp = nq // nkv
    SL = 17 * P
    offs = []
    o = 0
    for qc in range(NT):
        offs.append(o)
        o += (qc + 1) * P
    TOT = o

    qb = p.sb("qb", [P, T], BF16)
    kb = p.sb("kb", [P, T], BF16)
    vb = p.sb("vb", [P, NT * P], BF16)
    idn = p.sb("idn", [P, P], BF16)
    m0s = p.sb("m0s", [P, P], F32)
    b0s = p.sb("b0s", [P, T], F32)
    biasH = p.sb("biasH", [P, T], F32)
    sls = p.sb("sls", [P, 16], F32)
    s2a = p.sb("s2a", [P, TOT], F32)
    prb = p.sb("prb", [P, TOT], BF16)
    prT = p.sb("prT", [P, TOT], BF16)
    diag = p.sb("diag", [P, NT * P], BF16)
    aosb = p.sb("aosb", [P, T], BF16)
    negmax = p.sb("negmax", [P, NT], F32)
    sums = p.sb("sums", [P, NT], F32)
    rsum = p.sb("rsum", [P, NT], F32)
    psA = [p.ps(f"psA{i}", [P, 2048], F32) for i in range(2)]
    psO = [psA[i] for i in range(2)]

    s_i = p.sem("s_i")
    s_o = p.sem("s_o")
    nc0 = 2 + (2 if alibi else 0)

    g = nc.gpsimd; te = nc.tensor; s = nc.scalar; v = nc.vector; sy = nc.sync
    sy.dma_start(idn[:, :], ident_h[:, :]).then_inc(s_i, 16)
    s.dma_start(m0s[:, :], m0_h[:, :]).then_inc(s_i, 16)
    if alibi:
        sy.dma_start(b0s[:, :], b0_h[:, :]).then_inc(s_i, 16)
        s.dma_start(sls[:, :], sl_h[:, :]).then_inc(s_i, 16)
    sy.wait_ge(s_i, 16 * nc0)
    if not alibi:
        v.memset(biasH[:, :], 0.0)
    nc.all_engine_barrier()

    def qk(qc):
        lk = (qc + 1) * P
        pp = qc % 2
        for n in range((lk + 511) // 512):
            w_ = min(512, lk - n * 512)
            te.matmul(psA[pp][:, n * 512:n * 512 + w_],
                      qb[:, qc * P:(qc + 1) * P], kb[:, n * 512:n * 512 + w_],
                      start=True, stop=True)

    def s2evac(qc):
        lk = (qc + 1) * P
        pp = qc % 2
        o_ = offs[qc]
        v.tensor_tensor(s2a[:, o_:o_ + lk], psA[pp][:, 0:lk], biasH[:, 0:lk], ALU.add)
        v.tensor_tensor(s2a[:, o_ + lk - P:o_ + lk], s2a[:, o_ + lk - P:o_ + lk],
                        m0s[:, :], ALU.add)
        v.reduce_max(negmax[:, qc:qc + 1], s2a[:, o_:o_ + lk], axis=AX.X, negate=True)

    def tmm(qc):
        pp = qc % 2
        o_ = offs[qc]
        for kc in range(qc + 1):
            te.matmul(psA[pp][:, kc * P:(kc + 1) * P],
                      prb[:, o_ + kc * P:o_ + (kc + 1) * P],
                      diag[:, qc * P:(qc + 1) * P], start=True, stop=True)

    def tevac(qc):
        lk = (qc + 1) * P
        pp = qc % 2
        o_ = offs[qc]
        v.tensor_copy(prT[:, o_:o_ + lk], psA[pp][:, 0:lk])

    def av(qc):
        pp = qc % 2
        o_ = offs[qc]
        for kc in range(qc + 1):
            te.matmul(psO[pp][:, 0:P], vb[:, kc * P:(kc + 1) * P],
                      prT[:, o_ + kc * P:o_ + (kc + 1) * P],
                      start=(kc == 0), stop=(kc == qc))

    def oevac(qc):
        pp = qc % 2
        s.copy(aosb[:, qc * P:(qc + 1) * P], psO[pp][:, 0:P])

    def head_body(h, kv, hterms, hconst):
        sy.dma_start(qb[:, :], qf[ts(h, P), :]).then_inc(s_i, 16)
        s.dma_start(kb[:, :], kf[ts(kv, P), :]).then_inc(s_i, 16)
        s.dma_start(vb[:, :].rearrange("p (c k) -> p c k", c=NT),
                    vtm.rearrange("a (c p) k -> a p c k", p=P)[kv, :, :, :]
                    ).then_inc(s_i, 16)
        sy.wait_ge(s_i, wval(p, sy, [(t_, 48 * c_) for (t_, c_) in hterms],
                             16 * (nc0 + 3 * hconst + 3)))
        nc.all_engine_barrier()
        if alibi:
            v.tensor_scalar(biasH[:, :], b0s[:, :], sls[:, ts(h, 1)], 1.0,
                            ALU.mult, ALU.mult)
            nc.all_engine_barrier()
        qk(0)
        nc.all_engine_barrier()
        for qc in range(1, NT):
            qk(qc)
            if ATTN_SERIAL:
                nc.all_engine_barrier()
            s2evac(qc - 1)
            nc.all_engine_barrier()
        s2evac(NT - 1)
        nc.all_engine_barrier()
        for qc in range(NT):
            lk = (qc + 1) * P
            o_ = offs[qc]
            s.activation(prb[:, o_:o_ + lk], s2a[:, o_:o_ + lk], AF.Exp,
                         bias=negmax[:, qc:qc + 1])
        nc.all_engine_barrier()
        for qc in range(NT):
            lk = (qc + 1) * P
            o_ = offs[qc]
            v.reduce_sum(sums[:, qc:qc + 1], prb[:, o_:o_ + lk], axis=AX.X)
        nc.all_engine_barrier()
        v.reciprocal(rsum[:, :], sums[:, :])
        nc.all_engine_barrier()
        for qc in range(NT):
            v.tensor_scalar(diag[:, qc * P:(qc + 1) * P], idn[:, :],
                            rsum[:, qc:qc + 1], 1.0, ALU.mult, ALU.mult)
        nc.all_engine_barrier()
        tmm(0)
        nc.all_engine_barrier()
        for qc in range(1, NT):
            tmm(qc)
            if ATTN_SERIAL:
                nc.all_engine_barrier()
            tevac(qc - 1)
            nc.all_engine_barrier()
        tevac(NT - 1)
        nc.all_engine_barrier()
        av(0)
        nc.all_engine_barrier()
        for qc in range(1, NT):
            av(qc)
            if ATTN_SERIAL:
                nc.all_engine_barrier()
            oevac(qc - 1)
            nc.all_engine_barrier()
        oevac(NT - 1)
        nc.all_engine_barrier()
        s.dma_start(ao[ts(h, P), :], aosb[:, :]).then_inc(s_o, 16)
        sy.wait_ge(s_o, wval(p, sy, [(t_, 16 * c_) for (t_, c_) in hterms],
                             16 * hconst + 16))
        nc.all_engine_barrier()

    if grp == 1:
        with myfori(p, ALLE, 0, nq) as h:
            head_body(h, h, [(h, 1)], 0)
    else:
        with myfori(p, ALLE, 0, nkv, 0) as kv:
            with myfori(p, ALLE, 0, grp, 1) as gq:
                h = kv * grp + gq
                head_body(h, kv, [(kv, grp), (gq, 1)], 0)


def build_attn(nq, nkv, alibi):
    p = Prog()
    qf = p.inp("qf", [nq * P, T], BF16)
    kf = p.inp("kf", [nkv * P, T], BF16)
    vtm = p.inp("vtm", [nkv, T, P], BF16)
    ident_h = p.inp("ident", [P, P], BF16)
    m0_h = p.inp("m0", [P, P], F32)
    ao = p.out("ao", [nq * P, T], BF16)
    b0_h = p.inp("b0", [P, T], F32) if alibi else None
    sl_h = p.inp("slopes", [P, 16], F32) if alibi else None
    emit_attn(p, nq, nkv, alibi, qf, kf, vtm, ident_h, m0_h, ao, b0_h, sl_h)
    return p


def build_fused():
    p = Prog()
    nc = p.nc
    NH = HID // P
    x = p.inp("x", [T, D], F32)
    wq0 = p.inp("wq0", [D, D], F32)
    wk0 = p.inp("wk0", [D, D], F32)
    wv0 = p.inp("wv0", [D, D], F32)
    bq0 = p.inp("bq0", [P, 16], F32)
    bk0 = p.inp("bk0", [P, 16], F32)
    vb0 = p.inp("vb0", [P, D], F32)
    ow0 = p.inp("ow0", [D, D], F32)
    ob0_h = p.inp("ob0", [P, 16], F32)
    sl = p.inp("sl", [P, 16], F32)
    b0c = p.inp("b0c", [P, T], F32)
    m0 = p.inp("m0", [P, P], F32)
    ident = p.inp("ident", [P, P], BF16)
    h128 = p.inp("h128", [P, P], BF16)
    wg0 = p.inp("wg0", [D, HID], I8)
    wu0 = p.inp("wu0", [D, HID], I8)
    wd0 = p.inp("wd0", [HID, D], I8)
    sg = p.inp("sg", [P, NH], F32)
    su = p.inp("su", [P, NH], F32)
    sd = p.inp("sd", [P, ND], F32)
    wq1 = p.inp("wq1", [D, D], F32)
    wk1 = p.inp("wk1", [D, 256], F32)
    wv1 = p.inp("wv1", [D, 256], F32)
    wo1 = p.inp("wo1", [D, D], F32)
    bq1 = p.inp("bq1", [P, 16], F32)
    bk1 = p.inp("bk1", [P, 2], F32)
    vb1 = p.inp("vb1", [P, 256], F32)
    cc = p.inp("cc", [P, T], F32)
    ss = p.inp("ss", [P, T], F32)
    ccq = p.inp("ccq", [P, T], F32)
    ssq = p.inp("ssq", [P, T], F32)
    w1t = p.inp("w1t", [D, HID], F32)
    w3t = p.inp("w3t", [D, HID], F32)
    w2t = p.inp("w2t", [HID, D], F32)
    xn = p.intern("xn_s", [D, T], BF16)
    qf = p.intern("qf_s", [D, T], BF16)
    kfA = p.intern("kfA_s", [D, T], BF16)
    kfB = p.intern("kfB_s", [256, T], BF16)
    vtA = p.intern("vtA_s", [16, T, P], BF16)
    vtB = p.intern("vtB_s", [2, T, P], BF16)
    ao = p.intern("ao_s", [D, T], BF16)
    yb = p.intern("yb_s", [D, T], BF16)
    fuse = p.intern("fuse_s", [HID, T], BF16)
    w2pt = p.intern("w2pt_s", [HID, D], BF16)
    x2 = p.intern("x2_s", [T, D], F32)
    x3 = p.intern("x3_s", [T, D], F32)
    x4 = p.intern("x4_s", [T, D], F32)
    xo = p.out("xo", [T, D], F32)

    phases = [
        ("w2p", lambda: emit_w2prep(p, w2t, h128, w2pt)),
        ("bf", lambda: emit_boundary(p, "first", x, None, ident, None, xn)),
        ("q0", lambda: emit_mm(p, xn, wq0, qf, bq0, None, None, None,
                               D, 16, "bias_scale", scale_imm=SC)),
        ("k0", lambda: emit_mm(p, xn, wk0, kfA, bk0, None, None, None,
                               D, 16, "bias_scale")),
        ("v0", lambda: emit_vtm(p, xn, wv0, vb0, vtA, 2048, 16)),
        ("a0", lambda: emit_attn(p, 16, 16, True, qf, kfA, vtA, ident, m0,
                                 ao, b0c, sl)),
        ("o0", lambda: emit_mm(p, ao, ow0, yb, ob0_h, None, None, None,
                               D, 16, "bias_scale")),
        ("b1", lambda: emit_boundary(p, "mid", x, yb, ident, x2, xn)),
        ("g0", lambda: emit_gateup(p, xn, wg0, wu0, fuse, sg, su)),
        ("d0", lambda: emit_down(p, fuse, wd0, yb, sd)),
        ("b2", lambda: emit_boundary(p, "mid", x2, yb, ident, x3, xn)),
        ("q1", lambda: emit_mm(p, xn, wq1, qf, bq1, None, ccq, ssq,
                               D, 16, "rope")),
        ("k1", lambda: emit_mm(p, xn, wk1, kfB, bk1, None, cc, ss,
                               D, 2, "rope")),
        ("v1", lambda: emit_vtm(p, xn, wv1, vb1, vtB, 256, 2)),
        ("a1", lambda: emit_attn(p, 16, 2, False, qf, kfB, vtB, ident, m0,
                                 ao, None, None)),
        ("o1", lambda: emit_mm(p, ao, wo1, yb, None, None, None, None,
                               D, 16, "plain")),
        ("b3", lambda: emit_boundary(p, "mid", x3, yb, ident, x4, xn)),
        ("g1", lambda: emit_gateup(p, xn, w1t, w3t, fuse, None, None)),
        ("d1", lambda: emit_down(p, fuse, w2pt, yb, None)),
        ("b4", lambda: emit_boundary(p, "final", x4, yb, ident, xo, None)),
    ]
    with p.es:
        for tag, fn in phases:
            p.begin(tag + "_")
            fn()
            nc.all_engine_barrier()
            p.end()
    return p


_PROGS = {}


def _get(name, fn):
    if name not in _PROGS:
        _PROGS[name] = fn()
    return _PROGS[name]


def _b2(a):
    n = a.shape[0] // P
    return np.ascontiguousarray(a.reshape(n, P).T).astype(np.float32)


def kernel(**inp):
    C = _consts()
    x0 = np.ascontiguousarray(np.asarray(inp["L0_attn_hidden_states"], np.float32))

    qkv_wT = np.ascontiguousarray(np.asarray(inp["L0_attn_qkv_w"], np.float32).T)
    qkvb2 = _b2(np.asarray(inp["L0_attn_qkv_b"], np.float32))
    com = {
        "wq0": np.ascontiguousarray(qkv_wT[:, 0:2048]),
        "wk0": np.ascontiguousarray(qkv_wT[:, 2048:4096]),
        "wv0": np.ascontiguousarray(qkv_wT[:, 4096:6144]),
        "bq0": np.ascontiguousarray(qkvb2[:, 0:16]),
        "bk0": np.ascontiguousarray(qkvb2[:, 16:32]),
        "vb0": np.broadcast_to(np.asarray(inp["L0_attn_qkv_b"], np.float32)[4096:],
                               (P, D)).copy(),
        "ow0": np.ascontiguousarray(np.asarray(inp["L0_attn_out_w"], np.float32).T),
        "ob0": _b2(np.asarray(inp["L0_attn_out_b"], np.float32)),
        "sl": np.broadcast_to(np.asarray(inp["L0_attn_alibi_slopes"], np.float32),
                              (P, 16)).copy(),
        "b0c": C["b0"], "m0": C["m0"], "ident": np.asarray(C["ident"]),
        "h128": np.asarray(C["h128"]),
        "wg0": np.ascontiguousarray(np.asarray(inp["L0_ffn_w_gate"], np.int8)),
        "wu0": np.ascontiguousarray(np.asarray(inp["L0_ffn_w_up"], np.int8)),
        "wd0": np.ascontiguousarray(np.asarray(inp["L0_ffn_w_down"], np.int8)),
        "sg": _b2(np.asarray(inp["L0_ffn_s_gate"], np.float32)),
        "su": _b2(np.asarray(inp["L0_ffn_s_up"], np.float32)),
        "sd": _b2(np.asarray(inp["L0_ffn_s_down"], np.float32)),
        "wq1": np.ascontiguousarray(np.asarray(inp["L1_attn_wq"], np.float32).T),
        "wk1": np.ascontiguousarray(np.asarray(inp["L1_attn_wk"], np.float32).T),
        "wv1": np.ascontiguousarray(np.asarray(inp["L1_attn_wv"], np.float32).T),
        "wo1": np.ascontiguousarray(np.asarray(inp["L1_attn_wo"], np.float32).T),
        "bq1": _b2(np.asarray(inp["L1_attn_bq"], np.float32)),
        "bk1": _b2(np.asarray(inp["L1_attn_bk"], np.float32)),
        "vb1": np.broadcast_to(np.asarray(inp["L1_attn_bv"], np.float32),
                               (P, 256)).copy(),
        "cc": C["cc"], "ss": C["ss"], "ccq": C["ccq"], "ssq": C["ssq"],
        "w1t": np.ascontiguousarray(np.asarray(inp["L1_ffn_W1"], np.float32).T),
        "w3t": np.ascontiguousarray(np.asarray(inp["L1_ffn_W3"], np.float32).T),
        "w2t": np.ascontiguousarray(np.asarray(inp["L1_ffn_W2"], np.float32).T),
    }
    pr = _get("fused", build_fused)
    in_maps = [dict(com, x=x0[b]) for b in range(NCORES)]
    res = run_bass_kernel_spmd(pr.nc, in_maps, core_ids=list(range(NCORES)))
    return np.stack([res.results[b]["xo"] for b in range(NCORES)],
                    axis=0).astype(np.float32)
